# version 10
# baseline (speedup 1.0000x reference)
import numpy as np
import concourse.bass as bass
import concourse.mybir as mybir
from concourse.bass_utils import run_bass_kernel_spmd

F32 = mybir.dt.float32
BF16 = mybir.dt.bfloat16
AF = mybir.ActivationFunctionType
ALU = mybir.AluOpType
AX = mybir.AxisListType

NCORES = 8
TCORE = 2048
C = 1024
KCH = 8
TILES = 16
E = 8
TSEQ = 4096
NCHUNK = 32
BIG = 1.0e6
EPS = 1e-5


class Ctr:
    __slots__ = ("v",)
    def __init__(self):
        self.v = 0
    def add(self, n=1):
        self.v += n
        return self.v


class PB:
    __slots__ = ("ap", "sem", "val")
    def __init__(self, ap):
        self.ap = ap
        self.sem = None
        self.val = 0
    def acq(self, t):
        if self.sem is not None and self.val:
            t.wait_ge(self.sem, self.val)
    def rel(self, sem, val):
        self.sem = sem
        self.val = val


def build_nc(upto="full"):
    nc = bass.Bass("TRN2", target_bir_lowering=False, debug=False)

    dt = lambda n, s: nc.dram_tensor(n, s, F32, kind="ExternalInput").ap()
    xin = dt("xshard", [TCORE, C])
    parity_in = dt("parity", [128, 1])
    ident_in = dt("ident128", [128, 128])
    tril_in = dt("tril128", [128, 128])
    iota8_in = dt("iota8b", [128, 8])
    iotaB_in = dt("iotaBb", [128, 8])
    iotap_in = dt("iotap", [128, 1])
    eps_in = dt("epsc", [128, 1])
    onec_in = dt("onec", [128, 1])
    ones_in = dt("ones1x128", [1, 128])
    id8_in = dt("ident8", [8, 8])
    gw_in = dt("gw_t128", [128, 64])
    gwp_in = dt("gatesw_p", [128, KCH * 3072])
    gb_in = dt("gates_b", [1, 3072])
    qbc_in = dt("qbc", [128, 8])
    lrw_in = dt("lrw_c", [128, 8])
    lrb_in = dt("lr_b", [1, 1])
    owp_in = dt("outw_p", [128, KCH * 1024])
    ob_in = dt("out_b", [1, 1024])
    w1p_in = dt("w1_p", [128, KCH * 1024])
    b1c_in = dt("b1c", [128, 8])
    w2p_in = dt("w2_p", [128, KCH * 1024])
    b2_in = dt("b2", [1, 1024])
    dng_in = dt("dn_g_c", [128, 8])
    dnb_in = dt("dn_b_c", [128, 8])
    mng_in = dt("mn_g_c", [128, 8])
    mnb_in = dt("mn_b_c", [128, 8])
    sim_in = dt("simm", [8, 8])
    gup_in = dt("gup_pk", [E, 16, 128, KCH * 128])
    dwn_in = dt("down_pk", [E, 128, 8 * 1024])

    di = lambda n, s: nc.dram_tensor(n, s, F32, kind="Internal").ap()
    xA = di("xA", [TCORE, C])
    xB = di("xB", [TCORE, C])
    zD = di("zD", [TCORE, C])
    z1D = di("z1D", [TCORE, C])
    qD = di("qD", [16, 128, KCH, 128])
    kvin = di("kvin", [TCORE, 2049])
    kvoutA = di("kvoutA", [TCORE, 2049])
    kvoutB = di("kvoutB", [TCORE, 2049])
    gtin = di("gtin", [512, C])
    gtout = di("gtout", [C, C])
    arin1 = di("arin1", [1, 8])
    arout1 = di("arout1", [1, 8])
    arin2 = di("arin2", [1, 8])
    arout2 = di("arout2", [1, 8])

    yout = nc.dram_tensor("y", [TCORE, C], F32, kind="ExternalOutput").ap()

    sb = lambda n, w: nc.alloc_sbuf_tensor(n, [128, w], F32).ap()
    ident = sb("ident", 128)
    tril = sb("tril", 128)
    iota8 = sb("iota8", 8)
    iotaB = sb("iotaB", 8)
    iotap = sb("iotap_s", 1)
    epsc = sb("epsc_s", 1)
    onec = sb("onec_s", 1)
    par = sb("par_s", 1)
    onesr = nc.alloc_sbuf_tensor("onesr", [1, 128], F32).ap()
    id8 = nc.alloc_sbuf_tensor("id8s", [8, 8], F32).ap()
    simm = nc.alloc_sbuf_tensor("simms", [8, 8], F32).ap()
    gwt = sb("gwt", 64)
    gwtH = nc.alloc_sbuf_tensor("gwtH", [128, 64], BF16).ap()
    gwtL = nc.alloc_sbuf_tensor("gwtL", [128, 64], BF16).ap()
    qbc = sb("qbc_s", 8)
    lrw = sb("lrw_s", 8)
    b1c = sb("b1c_s", 8)
    dng = sb("dng", 8)
    dnb = sb("dnb", 8)
    mng = sb("mng", 8)
    mnb = sb("mnb", 8)
    gbb = sb("gbb", 2048)
    obb = sb("obb", 1024)
    b2b = sb("b2b", 1024)
    lrbb = sb("lrbb", 1)
    rt = sb("rt", 64)
    comb = sb("combs", 128)
    mapb = sb("mapb", 8)
    cntacc = sb("cntacc", 8)
    srow = nc.alloc_sbuf_tensor("srow", [1, 8], F32).ap()
    scol = nc.alloc_sbuf_tensor("scol", [8, 1], F32).ap()
    s8a = nc.alloc_sbuf_tensor("s8a", [8, 8], F32).ap()
    s8b = nc.alloc_sbuf_tensor("s8b", [8, 8], F32).ap()
    s8c = nc.alloc_sbuf_tensor("s8c", [8, 8], F32).ap()
    c81 = nc.alloc_sbuf_tensor("c81", [8, 1], F32).ap()
    c82 = nc.alloc_sbuf_tensor("c82", [8, 1], F32).ap()
    c83 = nc.alloc_sbuf_tensor("c83", [8, 1], F32).ap()
    bnos = sb("bnos", 12)
    bnag = sb("bnag", 2)
    st1 = sb("st1", 1)
    st2 = sb("st2", 1)
    st3 = sb("st3", 1)
    st4 = sb("st4", 1)

    ARENA_W = 46080
    arena = nc.alloc_sbuf_tensor("arena", [128, ARENA_W], F32).ap()
    def A(off, w):
        assert 0 <= off and off + w <= ARENA_W, (off, w)
        return arena[:, off:off + w]

    def AB(f32_base, bidx, bw):
        assert bidx % 2 == 0 and bw % 2 == 0
        off = f32_base + bidx // 2
        assert off + bw // 2 <= ARENA_W, (f32_base, bidx, bw)
        return arena[:, off:off + bw // 2].bitcast(BF16)

    pbt = [nc.alloc_psum_tensor(f"pb{i}", [128, 512], F32).ap() for i in range(8)]
    PBK = [PB(pbt[i]) for i in range(8)]

    import contextlib
    stack = contextlib.ExitStack()
    semW = stack.enter_context(nc.semaphore("semW")); cW = Ctr()
    semG = stack.enter_context(nc.semaphore("semG")); cG = Ctr()
    semT = stack.enter_context(nc.semaphore("semT")); cT = Ctr()
    semV = stack.enter_context(nc.semaphore("semV")); cV = Ctr()
    semS = stack.enter_context(nc.semaphore("semS")); cS = Ctr()
    semC = stack.enter_context(nc.semaphore("semC")); cC = Ctr()
    semB = stack.enter_context(nc.semaphore("semB")); cB = Ctr()
    semP = stack.enter_context(nc.semaphore("semP")); cP = Ctr()
    blk = stack.enter_context(nc.Block())

    def sT(f): blk.tensor(f)
    def sV(f): blk.vector(f)
    def sS(f): blk.scalar(f)
    def sG(f): blk.gpsimd(f)
    def sY(f): blk.sync(f)

    def barrier():
        tgt = cB.add(5)
        for sec in (sT, sV, sS, sG, sY):
            def f(e, _t=tgt):
                e.drain()
                e.nop().then_inc(semB, 1)
                e.wait_ge(semB, _t)
            sec(f)

    def gD(g, out, in_, slow=False):
        g.wait_ge(semG, cG.v)
        g.dma_start(out=out, in_=in_, allow_slow_non_contiguous=slow).then_inc(semG, 16)
        cG.add(16)

    def yD(y, out, in_):
        y.wait_ge(semW, cW.v)
        y.dma_start(out=out, in_=in_).then_inc(semW, 16)
        cW.add(16)

    def f(y):
        for dst, src in [(ident, ident_in), (tril, tril_in), (iota8, iota8_in),
                         (iotaB, iotaB_in), (iotap, iotap_in), (epsc, eps_in),
                         (par, parity_in), (onesr, ones_in), (id8, id8_in), (onec, onec_in),
                         (simm, sim_in), (gwt, gw_in), (qbc, qbc_in),
                         (lrw, lrw_in), (b1c, b1c_in), (dng, dng_in),
                         (dnb, dnb_in), (mng, mng_in), (mnb, mnb_in)]:
            yD(y, dst, src)
        yD(y, arena[0:1, 0:3072], gb_in)
        yD(y, arena[0:1, 3072:4096], ob_in)
        yD(y, arena[0:1, 4096:5120], b2_in)
        yD(y, arena[0:1, 5120:5121], lrb_in)
    sY(f)

    def f(t):
        t.wait_ge(semW, cW.v)
        for j in range(4):
            t.matmul(PBK[j].ap[:, 0:512], onesr,
                     arena[0:1, 1024 + j * 512:1024 + (j + 1) * 512], start=True, stop=True)
        for j in range(2):
            t.matmul(PBK[4 + j].ap[:, 0:512], onesr,
                     arena[0:1, 3072 + j * 512:3072 + (j + 1) * 512], start=True, stop=True)
        for j in range(2):
            ins = t.matmul(PBK[6 + j].ap[:, 0:512], onesr,
                           arena[0:1, 4096 + j * 512:4096 + (j + 1) * 512], start=True, stop=True)
        ins.then_inc(semT, 1)
    sT(f); cT.add(1)
    def f(v, _t=cT.v):
        v.wait_ge(semT, _t)
        for j in range(4):
            v.tensor_copy(gbb[:, j * 512:(j + 1) * 512], PBK[j].ap[:, 0:512])
        for j in range(2):
            v.tensor_copy(obb[:, j * 512:(j + 1) * 512], PBK[4 + j].ap[:, 0:512])
        v.tensor_copy(b2b[:, 0:512], PBK[6].ap[:, 0:512])
        ins = v.tensor_copy(b2b[:, 512:1024], PBK[7].ap[:, 0:512])
        ins.then_inc(semV, 1)
    sV(f); cV.add(1)
    for b in PBK:
        b.rel(semV, cV.v)
    def f(t, _v=cV.v):
        t.wait_ge(semV, _v)
        t.matmul(PBK[0].ap[:, 0:1], onesr, arena[0:1, 5120:5121],
                 start=True, stop=True).then_inc(semT, 1)
    sT(f); cT.add(1)
    def f(v, _t=cT.v):
        v.wait_ge(semT, _t)
        v.tensor_copy(lrbb, PBK[0].ap[:, 0:1]).then_inc(semV, 1)
    sV(f); cV.add(1)
    PBK[0].rel(semV, cV.v)
    def f(v):
        v.drain()
        v.tensor_copy(gwtH, gwt)
        v.drain()
        v.tensor_tensor(gwtL, gwt, gwtH, ALU.subtract).then_inc(semV, 1)
    sV(f); cV.add(1)
    barrier()

    def ln_tile(x_ap, out_ap, sq_off, wait_g=0):
        def fv(v, _g=wait_g, _s=cS.v):
            if _g:
                v.wait_ge(semG, _g)
            v.wait_ge(semS, _s)
            v.drain()
            v.bn_stats(bnos[:, 0:6], x_ap[:, 0:512])
            v.drain()
            v.bn_stats(bnos[:, 6:12], x_ap[:, 512:1024])
            v.drain()
            v.bn_aggr(bnag, bnos).then_inc(semV, 1)
        sV(fv); cV.add(1)
        def fs(s, _v=cV.v):
            s.wait_ge(semV, _v)
            s.activation(st4, bnag[:, 1:2], AF.Sqrt, bias=epsc, scale=1.0).then_inc(semS, 1)
        sS(fs); cS.add(1)
        def fv(v, _s=cS.v):
            v.wait_ge(semS, _s)
            v.reciprocal(st4, st4)
            v.drain()
            v.tensor_scalar(out_ap, x_ap, bnag[:, 0:1], st4,
                            ALU.subtract, ALU.mult).then_inc(semV, 1)
        sV(fv); cV.add(1)
        return cV.v

    def transpose_tile(src_ap, dst_base, dst_stride, tile_idx, affine=None,
                       wait_v=0, banks=(5, 6), pair=None):
        outs = []
        for gi, bi in enumerate(banks):
            bank = PBK[bi]
            def ft(t, _g=gi, _b=bank, _w=wait_v):
                if _w:
                    t.wait_ge(semV, _w)
                _b.acq(t)
                for j in range(4):
                    kc = _g * 4 + j
                    ins = t.transpose(_b.ap[:, j * 128:(j + 1) * 128],
                                      src_ap[:, kc * 128:(kc + 1) * 128], ident)
                ins.then_inc(semT, 1)
            sT(ft); cT.add(1)
            ttgt = cT.v
            if pair is not None:
                hiB, loB, bstride = pair
                def fe(v, _g=gi, _b=bank, _t=ttgt):
                    v.wait_ge(semT, _t)
                    for j in range(4):
                        kc = _g * 4 + j
                        bidx = kc * bstride + tile_idx * 128
                        hi = AB(hiB, bidx, 128)
                        lo = AB(loB, bidx, 128)
                        if affine is None:
                            v.tensor_copy(hi, _b.ap[:, j * 128:(j + 1) * 128])
                            v.drain()
                            ins = v.tensor_tensor(lo, _b.ap[:, j * 128:(j + 1) * 128],
                                                  hi, ALU.subtract)
                            v.drain()
                        else:
                            g_ap, b_ap, scr = affine
                            v.tensor_scalar(A(scr, 128), _b.ap[:, j * 128:(j + 1) * 128],
                                            g_ap[:, kc:kc + 1], b_ap[:, kc:kc + 1],
                                            ALU.mult, ALU.add)
                            v.drain()
                            v.tensor_copy(hi, A(scr, 128))
                            v.drain()
                            ins = v.tensor_tensor(lo, A(scr, 128), hi, ALU.subtract)
                            v.drain()
                    ins.then_inc(semV, 1)
                sV(fe); cV.add(1)
                bank.rel(semV, cV.v)
                outs.append(cV.v)
                continue
            use_v = (gi == 0)
            def fe(e, _g=gi, _b=bank, _t=ttgt, _use_v=use_v):
                e.wait_ge(semT, _t)
                for j in range(4):
                    kc = _g * 4 + j
                    d0 = dst_base + kc * dst_stride + tile_idx * 128
                    dst = A(d0, 128)
                    if affine is None:
                        if _use_v:
                            ins = e.tensor_copy(dst, _b.ap[:, j * 128:(j + 1) * 128])
                        else:
                            ins = e.copy(dst, _b.ap[:, j * 128:(j + 1) * 128])
                    else:
                        g_ap, b_ap = affine
                        if _use_v:
                            ins = e.tensor_scalar(dst, _b.ap[:, j * 128:(j + 1) * 128],
                                                  g_ap[:, kc:kc + 1], b_ap[:, kc:kc + 1],
                                                  ALU.mult, ALU.add)
                        else:
                            ins = e.activation(dst, _b.ap[:, j * 128:(j + 1) * 128], AF.Identity,
                                               bias=b_ap[:, kc:kc + 1], scale=g_ap[:, kc:kc + 1])
                ins.then_inc(semV if _use_v else semS, 1)
            if use_v:
                sV(fe); cV.add(1)
                bank.rel(semV, cV.v)
                outs.append(cV.v)
            else:
                sS(fe); cS.add(1)
                bank.rel(semS, cS.v)
                outs.append(cS.v)
        return outs[0], outs[1]

    ZNH = 0
    ZNL = 8192
    OUTH = 16384
    HMID = 24576
    DOWN = 32768
    GB2 = 40960
    HST = 41984
    WSTG = 43008
    WPR = [44032, 45056]
    DWH = 32768
    DWL = 36864
    HMH = 24576
    HML = 28672
    XTS = 43008
    PPX = 24576
    PPZ = 28672
    PPSQ = 30720

    def ffn_phase(src_ap, dst_ap, arin, arout):
        stage_free = [0]
        wslot_free = [0, 0]
        dw_free = 0
        down_free = 0

        def stage_load(src, hi_off, lo_off, width_f32, free_wait):
            def fy(yq, _w=stage_free[0], _g=cG.v):
                if _w:
                    yq.wait_ge(semP, _w)
                yq.wait_ge(semG, _g)
                yD(yq, A(WSTG, width_f32), src)
            sY(fy)
            wtgt = cW.v
            def fs(sx, _w=wtgt, _free=free_wait):
                sx.wait_ge(semW, _w)
                if _free:
                    sx.wait_ge(semT, _free)
                ins = sx.activation(AB(hi_off, 0, width_f32), A(WSTG, width_f32),
                                    AF.Identity, bias=0.0, scale=1.0)
                ins.then_inc(semS, 1)
            sS(fs); cS.add(1)
            cast_t = cS.v
            def fgp(g, _c=cast_t):
                g.wait_ge(semS, _c)
                ins = g.tensor_tensor(AB(lo_off, 0, width_f32), A(WSTG, width_f32),
                                      AB(hi_off, 0, width_f32), ALU.subtract)
                ins.then_inc(semP, 1)
            sG(fgp); cP.add(1)
            stage_free[0] = cP.v
            return (cast_t, cP.v)

        pre_wsp = [stage_load(gup_in[0, 0], WPR[0], WPR[0] + 512, 1024, 0),
                   stage_load(gup_in[0, 8], WPR[1], WPR[1] + 512, 1024, 0)]

        def fv(v):
            v.memset(cntacc, 0.0).then_inc(semV, 1)
        sV(fv); cV.add(1)
        xfree = {}
        sig_last = [0]
        znv = [0, 0]
        for tt in range(TILES):
            slot = PPX + (tt % 2) * 2048
            wtgt = xfree.get(tt - 2, 0)
            def fg(g, _tt=tt, _s=slot, _w=wtgt):
                if _w:
                    g.wait_ge(semT, _w)
                gD(g, A(_s, 1024), src_ap[_tt * 128:(_tt + 1) * 128, :])
            sG(fg)
            ln_done = ln_tile(A(slot, 1024), A(PPZ, 1024), PPSQ, wait_g=cG.v)
            transpose_tile(A(PPZ, 1024), 0, 2048, tt, wait_v=ln_done,
                           pair=(ZNH, ZNL, 2048))
            lb = PBK[4 + 3 * (tt % 2)]
            ev_v = cV.v
            def ft(t, _tt=tt, _b=lb, _ev=ev_v):
                t.wait_ge(semV, _ev)
                _b.acq(t)
                n = 0
                for kc in range(KCH):
                    zh = AB(ZNH, kc * 2048 + _tt * 128, 128)
                    zl = AB(ZNL, kc * 2048 + _tt * 128, 128)
                    for lhs, rhs in ((zh, gwtH[:, kc * 8:(kc + 1) * 8]),
                                     (zh, gwtL[:, kc * 8:(kc + 1) * 8]),
                                     (zl, gwtH[:, kc * 8:(kc + 1) * 8])):
                        ins = t.matmul(_b.ap[:, 0:8], lhs, rhs,
                                       start=(n == 0), stop=(n == 3 * KCH - 1))
                        n += 1
                ins.then_inc(semT, 1)
            sT(ft); cT.add(1)
            xfree[tt] = cT.v
            ttgt = cT.v
            rtc = rt[:, tt * 4:tt * 4 + 4]
            def fv(v, _t=ttgt, _b=lb, _rtc=rtc):
                v.wait_ge(semT, _t)
                l = A(PPSQ, 8)
                v.tensor_copy(l, _b.ap[:, 0:8])
                v.drain()
                v.tensor_reduce(st1, l, AX.X, ALU.max)
                v.drain()
                ohr = A(PPSQ + 8, 8)
                v.tensor_scalar(ohr, l, st1, None, ALU.is_equal)
                v.drain()
                t1 = A(PPSQ + 16, 8)
                v.scalar_tensor_tensor(t1, ohr, -BIG, iotaB, ALU.mult, ALU.add)
                v.drain()
                v.tensor_reduce(_rtc[:, 0:1], t1, AX.X, ALU.min)
                v.drain()
                oh0 = A(PPSQ + 24, 8)
                v.tensor_scalar(oh0, iota8, _rtc[:, 0:1], None, ALU.is_equal)
                v.drain()
                v.tensor_add(cntacc, cntacc, oh0)
                v.drain()
                l2 = A(PPSQ + 32, 8)
                v.scalar_tensor_tensor(l2, oh0, -BIG, l, ALU.mult, ALU.add)
                v.drain()
                v.tensor_reduce(st2, l2, AX.X, ALU.max)
                v.drain()
                ohr2 = A(PPSQ + 40, 8)
                v.tensor_scalar(ohr2, l2, st2, None, ALU.is_equal)
                v.drain()
                t2 = A(PPSQ + 48, 8)
                v.scalar_tensor_tensor(t2, ohr2, -BIG, iotaB, ALU.mult, ALU.add)
                v.drain()
                v.tensor_reduce(_rtc[:, 1:2], t2, AX.X, ALU.min)
                v.drain()
                v.tensor_sub(st3, st1, st2).then_inc(semV, 1)
            sV(fv); cV.add(1)
            lb.rel(semV, cV.v)
            def fs(s, _v=cV.v, _rtc=rtc):
                s.wait_ge(semV, _v)
                s.activation(_rtc[:, 2:3], st3, AF.Sigmoid, bias=0.0, scale=1.0)
                ins = s.activation(_rtc[:, 3:4], st3, AF.Sigmoid, bias=0.0, scale=-1.0)
                ins.then_inc(semS, 1)
            sS(fs); cS.add(1)
            sig_last[0] = cS.v
            if tt == 7:
                znv[0] = cV.v
            if tt == 15:
                znv[1] = cV.v

        cnt_mm = PBK[4]
        def ft(t, _v=cV.v):
            t.wait_ge(semV, _v)
            cnt_mm.acq(t)
            t.matmul(cnt_mm.ap[0:1, 8:16], onec, cntacc, start=True, stop=True).then_inc(semT, 1)
        sT(ft); cT.add(1)
        def fv(v, _t=cT.v):
            v.wait_ge(semT, _t)
            v.tensor_copy(srow, cnt_mm.ap[0:1, 8:16]).then_inc(semV, 1)
        sV(fv); cV.add(1)
        cnt_mm.rel(semV, cV.v)
        def fg(g, _v=cV.v):
            g.wait_ge(semV, _v)
            gD(g, arin, srow)
            g.wait_ge(semG, cG.v)
            g.collective_compute("AllReduce", ALU.add,
                                 replica_groups=[list(range(NCORES))],
                                 ins=[arin], outs=[arout]).then_inc(semC, 1)
        sG(fg); cC.add(1)
        ar_done = cC.v
        def fg(g, _c=ar_done):
            g.wait_ge(semC, _c)
            gD(g, srow, arout)
            gD(g, scol, arout.rearrange("1 e -> e 1"), slow=True)
        sG(fg)
        def fv(v, _g=cG.v):
            v.wait_ge(semG, _g)
            v.tensor_scalar(srow, srow, 0.0, None, ALU.is_gt)
            ins = v.tensor_scalar(scol, scol, 0.0, None, ALU.is_gt)
            ins.then_inc(semV, 1)
        sV(fv); cV.add(1)
        def ft(t, _v=cV.v):
            t.wait_ge(semV, _v)
            cnt_mm.acq(t)
            t.matmul(cnt_mm.ap[0:8, 16:24], onesr[:, 0:8], srow,
                     start=True, stop=True).then_inc(semT, 1)
        sT(ft); cT.add(1)
        def fv(v, _t=cT.v):
            v.wait_ge(semT, _t)
            v.tensor_copy(s8a, cnt_mm.ap[0:8, 16:24])
            v.drain()
            v.tensor_mul(s8b, simm, s8a)
            v.drain()
            v.tensor_scalar(s8c, s8a, BIG, -BIG, ALU.mult, ALU.add)
            v.drain()
            v.tensor_add(s8b, s8b, s8c)
            v.drain()
            v.tensor_reduce(c81, s8b, AX.X, ALU.max)
            v.drain()
            v.tensor_scalar(s8c, s8b, c81, None, ALU.is_equal)
            v.drain()
            v.scalar_tensor_tensor(s8c, s8c, -BIG, iotaB[0:8, :], ALU.mult, ALU.add)
            v.drain()
            v.tensor_reduce(c82, s8c, AX.X, ALU.min)
            v.drain()
            v.tensor_scalar(c81, c81, 0.7, None, ALU.is_ge)
            v.drain()
            v.tensor_scalar(c83, scol, -1.0, 1.0, ALU.mult, ALU.add)
            v.drain()
            v.tensor_mul(c81, c81, c83)
            v.drain()
            v.tensor_mul(c82, c82, c81)
            v.drain()
            v.tensor_scalar(c83, c81, -1.0, 1.0, ALU.mult, ALU.add)
            v.drain()
            v.tensor_mul(c83, c83, iotap[0:8, :])
            v.drain()
            ins = v.tensor_add(c82, c82, c83)
            ins.then_inc(semV, 1)
        sV(fv); cV.add(1)
        cnt_mm.rel(semV, cV.v)
        def ft(t, _v=cV.v):
            t.wait_ge(semV, _v)
            cnt_mm.acq(t)
            t.matmul(cnt_mm.ap[0:1, 24:32], c82, id8, start=True, stop=True).then_inc(semT, 1)
        sT(ft); cT.add(1)
        def fv(v, _t=cT.v):
            v.wait_ge(semT, _t)
            v.tensor_copy(srow, cnt_mm.ap[0:1, 24:32]).then_inc(semV, 1)
        sV(fv); cV.add(1)
        cnt_mm.rel(semV, cV.v)
        def ft(t, _v=cV.v):
            t.wait_ge(semV, _v)
            cnt_mm.acq(t)
            t.matmul(cnt_mm.ap[:, 32:40], onesr, srow, start=True, stop=True).then_inc(semT, 1)
        sT(ft); cT.add(1)
        def fv(v, _t=cT.v, _sig=sig_last[0]):
            v.wait_ge(semT, _t)
            v.wait_ge(semS, _sig)
            v.tensor_copy(mapb, cnt_mm.ap[:, 32:40])
            v.drain()
            for tt2 in range(TILES):
                rtc = rt[:, tt2 * 4:tt2 * 4 + 4]
                oh = A(PPSQ, 8)
                t3 = A(PPSQ + 8, 8)
                v.tensor_scalar(oh, iota8, rtc[:, 0:1], None, ALU.is_equal)
                v.drain()
                v.tensor_mul(oh, oh, mapb)
                v.drain()
                v.tensor_reduce(st1, oh, AX.X, ALU.add)
                v.drain()
                v.tensor_scalar(oh, iota8, rtc[:, 1:2], None, ALU.is_equal)
                v.drain()
                v.tensor_mul(oh, oh, mapb)
                v.drain()
                v.tensor_reduce(st2, oh, AX.X, ALU.add)
                v.drain()
                v.tensor_scalar(oh, iota8, st1, rtc[:, 2:3], ALU.is_equal, ALU.mult)
                v.drain()
                v.tensor_scalar(t3, iota8, st2, rtc[:, 3:4], ALU.is_equal, ALU.mult)
                v.drain()
                ins = v.tensor_add(comb[:, tt2 * 8:(tt2 + 1) * 8], oh, t3)
                v.drain()
            ins.then_inc(semV, 1)
        sV(fv); cV.add(1)
        cnt_mm.rel(semV, cV.v)
        comb_ready = cV.v

        gslot_free = (semV, 0)
        mlist = [x for pr in zip(range(8), range(8, 16)) for x in pr]
        hmid_guard = comb_ready
        gu_pairs = [(PBK[0], PBK[1]), (PBK[4], PBK[7])]
        dn_pairs = [(PBK[2], PBK[3]), (PBK[0], PBK[1])]

        for half in range(2):
            for e in range(E):
                dn_loaded = {}
                g_done = 0
                hm_last = 0
                for mi, m in enumerate(mlist):
                    slot = mi % 2
                    if half == 0 and e == 0 and mi < 2:
                        wsp = pre_wsp[mi]
                    else:
                        wsp = stage_load(gup_in[e, m], WPR[slot], WPR[slot] + 512,
                                         1024, wslot_free[slot])
                    if mi >= 8:
                        mf_d = mi - 8
                        dn_loaded[mf_d] = stage_load(
                            dwn_in[e][:, mf_d * 1024:(mf_d + 1) * 1024],
                            DWH + mf_d * 512, DWL + mf_d * 512, 1024, dw_free)
                    bks = gu_pairs[mi % 2]
                    def ft(t, _s=slot, _w=wsp, _zn=znv[half], _b=bks, _half=half):
                        t.wait_ge(semS, _w[0])
                        t.wait_ge(semP, _w[1])
                        t.wait_ge(semV, _zn)
                        _b[0].acq(t)
                        _b[1].acq(t)
                        n = [0, 0]
                        for kc in range(KCH):
                            wh = AB(WPR[_s], kc * 128, 128)
                            wl = AB(WPR[_s] + 512, kc * 128, 128)
                            zh = [AB(ZNH, kc * 2048 + _half * 1024 + nn * 512, 512) for nn in range(2)]
                            zl = [AB(ZNL, kc * 2048 + _half * 1024 + nn * 512, 512) for nn in range(2)]
                            for lhs, which in ((wh, "h"), (wl, "l")):
                                for nn in range(2):
                                    movs = (zh[nn], zl[nn]) if which == "h" else (zh[nn],)
                                    for rhs in movs:
                                        ins = t.matmul(_b[nn].ap, lhs, rhs,
                                                       start=(n[nn] == 0),
                                                       stop=(n[nn] == 3 * KCH - 1))
                                        n[nn] += 1
                        ins.then_inc(semT, 1)
                    sT(ft); cT.add(1)
                    wslot_free[slot] = cT.v
                    mmtgt = cT.v
                    if m < 8:
                        gw_ = gslot_free
                        def fs(s, _t=mmtgt, _b=bks, _gw=gw_):
                            s.wait_ge(semT, _t)
                            if _gw[1]:
                                s.wait_ge(_gw[0], _gw[1])
                            s.activation(A(GB2, 512), _b[0].ap, AF.Silu, bias=0.0, scale=1.0)
                            ins = s.activation(A(GB2 + 512, 512), _b[1].ap, AF.Silu,
                                               bias=0.0, scale=1.0)
                            ins.then_inc(semS, 1)
                        sS(fs); cS.add(1)
                        bks[0].rel(semS, cS.v)
                        bks[1].rel(semS, cS.v)
                        g_done = cS.v
                    else:
                        mf = m - 8
                        def fv(v, _mf=mf, _t=mmtgt, _b=bks, _gd=g_done, _hg=hmid_guard):
                            v.wait_ge(semT, _t)
                            v.wait_ge(semS, _gd)
                            if _hg:
                                v.wait_ge(semV, _hg)
                            for nn in range(2):
                                v.drain()
                                v.tensor_mul(A(HST, 512), A(GB2 + nn * 512, 512), _b[nn].ap)
                                v.drain()
                                v.tensor_copy(AB(HMH, _mf * 1024 + nn * 512, 512), A(HST, 512))
                                v.drain()
                                ins = v.tensor_tensor(AB(HML, _mf * 1024 + nn * 512, 512),
                                                      A(HST, 512),
                                                      AB(HMH, _mf * 1024 + nn * 512, 512),
                                                      ALU.subtract)
                            ins.then_inc(semV, 1)
                        sV(fv); cV.add(1)
                        bks[0].rel(semV, cV.v)
                        bks[1].rel(semV, cV.v)
                        hm_last = cV.v
                        gslot_free = (semV, cV.v)
                hmid_guard = 0
                hm_all = hm_last
                dn_s = max(v[0] for v in dn_loaded.values())
                dn_p = max(v[1] for v in dn_loaded.values())
                for tc in range(8):
                    bks = dn_pairs[tc % 2]
                    def ft(t, _tc=tc, _b=bks, _hm=hm_all, _ds=dn_s, _dp=dn_p):
                        t.wait_ge(semV, _hm)
                        t.wait_ge(semS, _ds)
                        t.wait_ge(semP, _dp)
                        _b[0].acq(t)
                        _b[1].acq(t)
                        n = [0, 0]
                        for mf in range(8):
                            hh = AB(HMH, mf * 1024 + _tc * 128, 128)
                            hl = AB(HML, mf * 1024 + _tc * 128, 128)
                            dh = [AB(DWH, mf * 1024 + nn * 512, 512) for nn in range(2)]
                            dl = [AB(DWL, mf * 1024 + nn * 512, 512) for nn in range(2)]
                            for lhs, which in ((hh, "h"), (hl, "l")):
                                for nn in range(2):
                                    movs = (dh[nn], dl[nn]) if which == "h" else (dh[nn],)
                                    for rhs in movs:
                                        ins = t.matmul(_b[nn].ap, lhs, rhs,
                                                       start=(n[nn] == 0),
                                                       stop=(n[nn] == 3 * KCH - 1))
                                        n[nn] += 1
                        ins.then_inc(semT, 1)
                    sT(ft); cT.add(1)
                    if tc == 7:
                        down_free = cT.v
                    mmtgt = cT.v
                    gtile = half * 8 + tc
                    ccol = comb[:, gtile * 8 + e:gtile * 8 + e + 1]
                    def fv(v, _t=mmtgt, _b=bks, _c=ccol, _tc=tc, _e=e, _cr=comb_ready):
                        v.wait_ge(semT, _t)
                        if _e == 0:
                            v.wait_ge(semV, _cr)
                        for nn in range(2):
                            v.drain()
                            d = OUTH + _tc * 1024 + nn * 512
                            if _e == 0:
                                ins = v.tensor_scalar(A(d, 512), _b[nn].ap, _c, None, ALU.mult)
                            else:
                                ins = v.scalar_tensor_tensor(A(d, 512), _b[nn].ap, _c,
                                                             A(d, 512), ALU.mult, ALU.add)
                        ins.then_inc(semV, 1)
                    sV(fv); cV.add(1)
                    bks[0].rel(semV, cV.v)
                    bks[1].rel(semV, cV.v)
                dw_free = down_free
                def fv(v, _h=down_free):
                    v.wait_ge(semT, _h)
                sV(fv)
            out_done = cV.v
            for tl in range(8):
                gtile = half * 8 + tl
                def fg(g, _gt=gtile, _t=cT.v, _v=cV.v):
                    g.wait_ge(semT, _t)
                    g.wait_ge(semV, _v)
                    gD(g, A(XTS, 1024), src_ap[_gt * 128:(_gt + 1) * 128, :])
                sG(fg)
                gt_ = cG.v
                def fv(v, _g=gt_, _tl=tl, _od=out_done):
                    v.wait_ge(semG, _g)
                    v.wait_ge(semV, _od)
                    v.drain()
                    ins = v.scalar_tensor_tensor(A(XTS, 1024), A(OUTH + _tl * 1024, 1024),
                                                 0.5, A(XTS, 1024), ALU.mult, ALU.add)
                    ins.then_inc(semV, 1)
                sV(fv); cV.add(1)
                vt_ = cV.v
                def fg(g, _v=vt_, _gt=gtile):
                    g.wait_ge(semV, _v)
                    gD(g, dst_ap[_gt * 128:(_gt + 1) * 128, :], A(XTS, 1024))
                sG(fg)
            barrier()

    ffn_phase(xin, xB, arin1, arout1)
    if upto == "ffn1":
        def fg(g):
            for tt in range(TILES):
                gD(g, yout[tt * 128:(tt + 1) * 128, :], xB[tt * 128:(tt + 1) * 128, :])
            g.wait_ge(semG, cG.v)
        sG(fg)
        stack.close()
        return nc

    GW = 0
    ZND = 24576
    AXT = 40960
    AZN = 43008
    ASQ = 45056
    def fy(yq):
        yD(yq, A(GW, 24576), gwp_in)
    sY(fy)
    gw_loaded = cW.v
    xfree = {}
    for tt in range(TILES):
        slot = AXT + (tt % 2) * 1024
        wtgt = xfree.get(tt - 2, 0)
        def fg(g, _tt=tt, _s=slot, _w=wtgt):
            if _w:
                g.wait_ge(semT, _w)
            gD(g, A(_s, 1024), xB[_tt * 128:(_tt + 1) * 128, :])
        sG(fg)
        zdone = ln_tile(A(slot, 1024), A(AZN, 1024), ASQ, wait_g=cG.v)
        def fg(g, _v=zdone, _tt=tt):
            g.wait_ge(semV, _v)
            gD(g, zD[_tt * 128:(_tt + 1) * 128, :], A(AZN, 1024))
        sG(fg)
        zst = cG.v
        nd = ln_tile(A(AZN, 1024), A(slot, 1024), ASQ, wait_g=zst)
        transpose_tile(A(slot, 1024), ZND, 2048, tt, affine=(dng, dnb), wait_v=nd)
        xfree[tt] = cT.v
    barrier()

    if upto == "attnpre":
        def fg(g):
            for tt in range(TILES):
                gD(g, yout[tt * 128:(tt + 1) * 128, :], zD[tt * 128:(tt + 1) * 128, :])
            g.wait_ge(semG, cG.v)
        sG(fg)
        stack.close()
        return nc

    AQV = 40960
    AKV = 43008
    AK2 = 45056
    def ft(t, _w=gw_loaded):
        t.wait_ge(semW, _w)
    sT(ft)
    qspill = {}
    for co in range(KCH):
        for nseg in range(4):
            bank = PBK[co % 2]
            def ft(t, _co=co, _n=nseg, _b=bank):
                _b.acq(t)
                for kc in range(KCH):
                    ins = t.matmul(_b.ap,
                                   arena[:, GW + kc * 3072 + _co * 128:GW + kc * 3072 + (_co + 1) * 128],
                                   arena[:, ZND + kc * 2048 + _n * 512:ZND + kc * 2048 + (_n + 1) * 512],
                                   start=(kc == 0), stop=(kc == KCH - 1))
                ins.then_inc(semT, 1)
            sT(ft); cT.add(1)
            mmt = cT.v
            qsl = AQV + (nseg % 2) * 512
            wq = qspill.get((co, nseg - 2) if nseg >= 2 else (co - 1, nseg + 2), 0)
            def fv(v, _t=mmt, _co=co, _b=bank, _q=qsl, _w=wq):
                v.wait_ge(semT, _t)
                if _w:
                    v.wait_ge(semW, _w)
                ins = v.tensor_scalar(A(_q, 512), _b.ap, qbc[:, _co:_co + 1], None, ALU.add)
                ins.then_inc(semV, 1)
            sV(fv); cV.add(1)
            bank.rel(semV, cV.v)
            ev = cV.v
            def fy(yq, _v=ev, _co=co, _n=nseg, _q=qsl):
                yq.wait_ge(semV, _v)
                yq.wait_ge(semW, cW.v) if False else None
                yD(yq, qD[_n * 4:(_n + 1) * 4, :, _co, :].rearrange("s p f -> p s f"),
                   arena[:, _q:_q + 512].rearrange("p (s f) -> p s f", s=4))
            sY(fy)
            qspill[(co, nseg)] = cW.v
    kvbuf_free = [0, 0]
    lr_banks = [PBK[5], PBK[6]]
    for tt in range(TILES):
        buf = AKV if tt % 2 == 0 else AK2
        bw = kvbuf_free[tt % 2]
        def ft(t, _tt=tt, _w=bw):
            if _w:
                t.wait_ge(semG, _w)
            PBK[2].acq(t)
            PBK[3].acq(t)
            for nh in range(2):
                for kc in range(KCH):
                    ins = t.matmul(PBK[2 + nh].ap,
                                   arena[:, ZND + kc * 2048 + _tt * 128:ZND + kc * 2048 + (_tt + 1) * 128],
                                   arena[:, GW + kc * 3072 + 1024 + nh * 512:GW + kc * 3072 + 1024 + (nh + 1) * 512],
                                   start=(kc == 0), stop=(kc == KCH - 1))
            ins.then_inc(semT, 1)
        sT(ft); cT.add(1)
        kmm = cT.v
        def fv(v, _t=kmm, _buf=buf):
            v.wait_ge(semT, _t)
            v.tensor_add(A(_buf, 512), PBK[2].ap, gbb[:, 0:512])
            v.tensor_add(A(_buf + 512, 512), PBK[3].ap, gbb[:, 512:1024])
            v.drain()
            v.tensor_mul(A(44032, 1024), A(_buf, 1024), A(_buf, 1024))
            v.drain()
            v.tensor_reduce(st1, A(44032, 1024), AX.X, ALU.add).then_inc(semV, 1)
        sV(fv); cV.add(1)
        PBK[2].rel(semV, cV.v)
        PBK[3].rel(semV, cV.v)
        def fs(s, _v=cV.v):
            s.wait_ge(semV, _v)
            s.activation(st2, st1, AF.Sqrt, bias=0.0, scale=1.0).then_inc(semS, 1)
        sS(fs); cS.add(1)
        def fv(v, _s=cS.v, _buf=buf):
            v.wait_ge(semS, _s)
            v.reciprocal(st2, st2)
            v.drain()
            ins = v.tensor_scalar(A(_buf, 1024), A(_buf, 1024), st2, None, ALU.mult)
            ins.then_inc(semV, 1)
        sV(fv); cV.add(1)
        knorm = cV.v
        def fg(g, _v=knorm, _tt=tt, _buf=buf):
            g.wait_ge(semV, _v)
            gD(g, kvin[_tt * 128:(_tt + 1) * 128, 0:1024], A(_buf, 1024))
        sG(fg)
        kst = cG.v
        def ft(t, _tt=tt, _g=kst):
            t.wait_ge(semG, _g)
            PBK[2].acq(t)
            PBK[3].acq(t)
            for nh in range(2):
                for kc in range(KCH):
                    ins = t.matmul(PBK[2 + nh].ap,
                                   arena[:, ZND + kc * 2048 + _tt * 128:ZND + kc * 2048 + (_tt + 1) * 128],
                                   arena[:, GW + kc * 3072 + 2048 + nh * 512:GW + kc * 3072 + 2048 + (nh + 1) * 512],
                                   start=(kc == 0), stop=(kc == KCH - 1))
            ins.then_inc(semT, 1)
        sT(ft); cT.add(1)
        vmm = cT.v
        def fv(v, _t=vmm, _buf=buf):
            v.wait_ge(semT, _t)
            v.tensor_add(A(_buf, 512), PBK[2].ap, gbb[:, 1024:1536])
            ins = v.tensor_add(A(_buf + 512, 512), PBK[3].ap, gbb[:, 1536:2048])
            ins.then_inc(semV, 1)
        sV(fv); cV.add(1)
        PBK[2].rel(semV, cV.v)
        PBK[3].rel(semV, cV.v)
        vev = cV.v
        def fg(g, _v=vev, _tt=tt, _buf=buf):
            g.wait_ge(semV, _v)
            gD(g, kvin[_tt * 128:(_tt + 1) * 128, 1024:2048], A(_buf, 1024))
        sG(fg)
        kvbuf_free[tt % 2] = cG.v
        lb = lr_banks[tt % 2]
        def ft(t, _tt=tt, _b=lb):
            _b.acq(t)
            for kc in range(KCH):
                ins = t.matmul(_b.ap[:, 0:1],
                               arena[:, ZND + kc * 2048 + _tt * 128:ZND + kc * 2048 + (_tt + 1) * 128],
                               lrw[:, kc:kc + 1],
                               start=(kc == 0), stop=(kc == KCH - 1))
            ins.then_inc(semT, 1)
        sT(ft); cT.add(1)
        lmm = cT.v
        def fv(v, _t=lmm, _b=lb):
            v.wait_ge(semT, _t)
            ins = v.tensor_add(A(AQV + 1024, 1), _b.ap[:, 0:1], lrbb)
            ins.then_inc(semV, 1)
        sV(fv); cV.add(1)
        lb.rel(semV, cV.v)
        lev = cV.v
        def fg(g, _v=lev, _tt=tt):
            g.wait_ge(semV, _v)
            gD(g, kvin[_tt * 128:(_tt + 1) * 128, 2048:2049], A(AQV + 1024, 1), slow=True)
            if _tt == 7:
                g.wait_ge(semG, cG.v)
                g.collective_compute("AllGather", ALU.bypass,
                                     replica_groups=[[0, 1], [2, 3], [4, 5], [6, 7]],
                                     ins=[kvin[0:1024, :]], outs=[kvoutA]).then_inc(semC, 1)
            if _tt == 15:
                g.wait_ge(semG, cG.v)
                g.collective_compute("AllGather", ALU.bypass,
                                     replica_groups=[[0, 1], [2, 3], [4, 5], [6, 7]],
                                     ins=[kvin[1024:2048, :]], outs=[kvoutB]).then_inc(semC, 1)
        sG(fg)
        if tt == 7:
            cC.add(1)
            kvA_ready = cC.v
        if tt == 15:
            cC.add(1)
            kvB_ready = cC.v
    barrier()

    if upto == "qkv":
        def fg(g):
            g.wait_ge(semC, kvB_ready)
            for tt in range(8):
                gD(g, yout[tt * 128:(tt + 1) * 128, 0:1024],
                   kvoutA[tt * 128:(tt + 1) * 128, 0:1024])
            for tt in range(8):
                gD(g, yout[(8 + tt) * 128:(9 + tt) * 128, 0:1024],
                   kvoutB[tt * 128:(tt + 1) * 128, 0:1024])
            g.wait_ge(semG, cG.v)
        sG(fg)
        stack.close()
        return nc

    S_ = 0
    GT_ = 4096
    KVT = 8192
    FMK2 = [14352, 15376, 16400]
    LPW2 = [17424, 19216]
    PCH2 = [21008, 21136, 21264]
    RB = 21392
    UB = 21904
    U2B = 22416
    VMB2 = [22928, 23440, 23952]
    OWPF = 24576
    def fy(yq):
        yD(yq, A(OWPF, 8192), owp_in)
    sY(fy)
    ow_loaded = cW.v

    kv_rel = {}
    pc_state = {}
    r_t = {}
    ld_t = {}
    vm_t = {}
    fmk_t = {}
    iat_t = {}
    u_t = {}
    s_ready = 0
    gt_last = 0

    def prep_load(j):
        sl = j % 3
        rel = kv_rel.get(j - 3, 0)
        rank, loc = j // 16, j % 16
        kv_src = (kvoutA if loc < 8 else kvoutB)
        kv_row = rank * 1024 + (loc % 8) * 128
        kv_sem = kvA_ready if loc < 8 else kvB_ready
        def fy(yq, _s=sl, _r=rel, _src=kv_src, _row=kv_row, _c=kv_sem):
            if _r:
                yq.wait_ge(semT, _r)
            yq.wait_ge(semC, _c)
            yD(yq, A(KVT + _s * 2052, 2049), _src[_row:_row + 128, 0:2049])
        sY(fy)
        ld_t[j] = cW.v
        vt0 = arena[:, KVT + sl * 2052 + 1024:KVT + sl * 2052 + 1536]
        vt1 = arena[:, KVT + sl * 2052 + 1536:KVT + sl * 2052 + 2048]
        def fv(v, _ld=ld_t[j], _v0=vt0, _v1=vt1, _m=VMB2[sl]):
            v.wait_ge(semW, _ld)
            v.drain()
            v.tensor_sub(A(_m, 512), _v1, _v0)
            v.drain()
            ins = v.scalar_tensor_tensor(A(_m, 512), A(_m, 512), par, _v0,
                                         ALU.mult, ALU.add)
            ins.then_inc(semV, 1)
        sV(fv); cV.add(1)
        vm_t[j] = cV.v

    def prep_compute1(j):
        sl = j % 3
        kt = arena[:, KVT + sl * 2052:KVT + sl * 2052 + 1024]
        FMK = FMK2[sl]
        LPW = LPW2[j % 2]
        fmk_v, fmk_s = 0, 0
        for gi, bi in enumerate((5, 6)):
            bank = PBK[bi]
            def ft(t, _g=gi, _b=bank, _kt=kt, _ld=ld_t[j]):
                t.wait_ge(semW, _ld)
                _b.acq(t)
                for jj in range(4):
                    kc = _g * 4 + jj
                    ins = t.transpose(_b.ap[:, jj * 128:(jj + 1) * 128],
                                      _kt[:, kc * 128:(kc + 1) * 128], ident)
                ins.then_inc(semT, 1)
            sT(ft); cT.add(1)
            ttgt = cT.v
            def fe(e, _g=gi, _b=bank, _t=ttgt, _F=FMK):
                e.wait_ge(semT, _t)
                for jj in range(4):
                    kc = _g * 4 + jj
                    if _g == 0:
                        ins = e.tensor_copy(A(_F + kc * 128, 128),
                                            _b.ap[:, jj * 128:(jj + 1) * 128])
                    else:
                        ins = e.copy(A(_F + kc * 128, 128),
                                     _b.ap[:, jj * 128:(jj + 1) * 128])
                ins.then_inc(semV if _g == 0 else semS, 1)
            if gi == 0:
                sV(fe); cV.add(1)
                bank.rel(semV, cV.v)
                fmk_v = cV.v
            else:
                sS(fe); cS.add(1)
                bank.rel(semS, cS.v)
                fmk_s = cS.v
        fmk_t[j] = (fmk_v, fmk_s)
        def ft(t, _fv=fmk_v, _fs=fmk_s, _F=FMK):
            t.wait_ge(semV, _fv)
            t.wait_ge(semS, _fs)
            PBK[4].acq(t)
            for kc in range(KCH):
                ins = t.matmul(PBK[4].ap[:, 0:128], A(_F + kc * 128, 128),
                               A(_F + kc * 128, 128),
                               start=(kc == 0), stop=(kc == KCH - 1))
            ins.then_inc(semT, 1)
        sT(ft); cT.add(1)
        def fv(v, _t=cT.v, _L=LPW):
            v.wait_ge(semT, _t)
            v.drain()
            ins = v.tensor_mul(A(_L, 128), PBK[4].ap[:, 0:128], tril)
            ins.then_inc(semV, 1)
        sV(fv); cV.add(1)
        PBK[4].rel(semV, cV.v)
        l_ready = cV.v
        def ft(t, _v=l_ready, _L=LPW):
            t.wait_ge(semV, _v)
            PBK[7].acq(t)
            t.transpose(PBK[7].ap[:, 0:128], A(_L, 128), ident).then_inc(semT, 1)
        sT(ft); cT.add(1)
        def fs(sx, _t=cT.v, _L=LPW):
            sx.wait_ge(semT, _t)
            sx.copy(A(_L + 128, 128), PBK[7].ap[:, 0:128]).then_inc(semS, 1)
        sS(fs); cS.add(1)
        PBK[7].rel(semS, cS.v)
        pc_state[j] = (l_ready, cS.v)

    pw_state = {}

    def pc2a_level(j, lev):
        LPW = LPW2[j % 2]
        if lev == 0:
            l_ready, u_ready = pc_state[j]
            pw_state[j] = (LPW, LPW + 128, l_ready, u_ready)
        prevL, prevU, pw_v, pw_s = pw_state[j]
        newL = LPW + (2 + 2 * lev) * 128
        newU = LPW + (3 + 2 * lev) * 128
        def ft(t, _pL=prevL, _pU=prevU, _v=pw_v, _s=pw_s):
            t.wait_ge(semV, _v)
            t.wait_ge(semS, _s)
            PBK[7].acq(t)
            PBK[4].acq(t)
            t.matmul(PBK[7].ap[:, 128:256], A(_pU, 128), A(_pL, 128),
                     start=True, stop=True)
            ins = t.matmul(PBK[4].ap[:, 128:256], A(_pL, 128), A(_pU, 128),
                           start=True, stop=True)
            ins.then_inc(semT, 1)
        sT(ft); cT.add(1)
        mmt = cT.v
        def fv(v, _t=mmt, _nL=newL):
            v.wait_ge(semT, _t)
            v.drain()
            ins = v.tensor_copy(A(_nL, 128), PBK[7].ap[:, 128:256])
            ins.then_inc(semV, 1)
        sV(fv); cV.add(1)
        PBK[7].rel(semV, cV.v)
        def fs(sx, _t=mmt, _nU=newU):
            sx.wait_ge(semT, _t)
            ins = sx.copy(A(_nU, 128), PBK[4].ap[:, 128:256])
            ins.then_inc(semS, 1)
        sS(fs); cS.add(1)
        PBK[4].rel(semS, cS.v)
        pw_state[j] = (newL, newU, cV.v, cS.v)

    def pc2b_level(j, lev):
        LPW = LPW2[j % 2]
        PCH = PCH2[j % 3]
        if lev == 0:
            _, u_ready = pc_state[j]
            def fv(v, _s=u_ready, _L=LPW, _P=PCH):
                v.wait_ge(semS, _s)
                v.drain()
                ins = v.tensor_sub(A(_P, 128), ident, A(_L + 128, 128))
                ins.then_inc(semV, 1)
            sV(fv); cV.add(1)
            iat_t[j] = cV.v
            return
        lv = lev - 1
        Lp = LPW + (2 + 2 * lv) * 128
        p_ready = iat_t[j]
        pw_v = pw_state[j][2]
        def ft(t, _Lp=Lp, _p=p_ready, _pv=pw_v, _P=PCH):
            t.wait_ge(semV, max(_p, _pv))
            PBK[7].acq(t)
            ins = t.matmul(PBK[7].ap[:, 256:384], A(_Lp, 128), A(_P, 128),
                           start=True, stop=True)
            ins.then_inc(semT, 1)
        sT(ft); cT.add(1)
        mmt = cT.v
        def fv(v, _t=mmt, _P=PCH):
            v.wait_ge(semT, _t)
            v.drain()
            ins = v.tensor_add(A(_P, 128), A(_P, 128), PBK[7].ap[:, 256:384])
            ins.then_inc(semV, 1)
        sV(fv); cV.add(1)
        PBK[7].rel(semV, cV.v)
        iat_t[j] = cV.v

    def main_kS(i):
        nonlocal s_ready
        sl = i % 3
        FMK = FMK2[sl]
        if i > 0:
            def ft(t, _sv=s_ready, _f=fmk_t[i], _F=FMK):
                t.wait_ge(semV, max(_sv, _f[0]))
                t.wait_ge(semS, _f[1])
                PBK[2].acq(t)
                for kc in range(KCH):
                    ins = t.matmul(PBK[2].ap, A(_F + kc * 128, 128), A(S_ + kc * 512, 512),
                                   start=(kc == 0), stop=(kc == KCH - 1))
                ins.then_inc(semT, 1)
            sT(ft); cT.add(1)
            ks = cT.v
            def fv(v, _t=ks, _vm=vm_t[i], _m=VMB2[sl]):
                v.wait_ge(semT, _t)
                v.wait_ge(semV, _vm)
                v.drain()
                ins = v.tensor_sub(A(RB, 512), A(_m, 512), PBK[2].ap)
                ins.then_inc(semV, 1)
            sV(fv); cV.add(1)
            PBK[2].rel(semV, cV.v)
        else:
            def fv(v, _vm=vm_t[i], _m=VMB2[sl]):
                v.wait_ge(semV, _vm)
                v.drain()
                ins = v.tensor_copy(A(RB, 512), A(_m, 512))
                ins.then_inc(semV, 1)
            sV(fv); cV.add(1)
        r_t[i] = cV.v

    def main_u(i):
        sl = i % 3
        def ft(t, _p=iat_t[i], _r=r_t[i], _P=PCH2[i % 3]):
            t.wait_ge(semV, max(_p, _r))
            PBK[1].acq(t)
            t.matmul(PBK[1].ap, A(_P, 128), A(RB, 512), start=True, stop=True).then_inc(semT, 1)
        sT(ft); cT.add(1)
        lrc = arena[:, KVT + sl * 2052 + 2048:KVT + sl * 2052 + 2049]
        def fv(v, _t=cT.v, _lr=lrc):
            v.wait_ge(semT, _t)
            v.drain()
            v.tensor_copy(A(UB, 512), PBK[1].ap)
            v.drain()
            ins = v.tensor_scalar(A(U2B, 512), A(UB, 512), _lr, None, ALU.mult)
            ins.then_inc(semV, 1)
        sV(fv); cV.add(1)
        PBK[1].rel(semV, cV.v)
        u_t[i] = cV.v

    def main_S(i, hb):
        nonlocal s_ready
        sl = i % 3
        kt = arena[:, KVT + sl * 2052:KVT + sl * 2052 + 1024]
        def ft(t, _hb=hb, _u=u_t[i], _kt=kt):
            t.wait_ge(semV, _u)
            for j in range(4):
                kc = _hb * 4 + j
                PBK[j].acq(t)
                ins = t.matmul(PBK[j].ap, _kt[:, kc * 128:(kc + 1) * 128], A(UB, 512),
                               start=True, stop=True)
            ins.then_inc(semT, 1)
        sT(ft); cT.add(1)
        mmt = cT.v
        def fv(v, _t=mmt, _hb=hb, _first=(i == 0)):
            v.wait_ge(semT, _t)
            v.drain()
            for j in range(4):
                kc = _hb * 4 + j
                if _first:
                    ins = v.tensor_copy(A(S_ + kc * 512, 512), PBK[j].ap)
                else:
                    ins = v.tensor_add(A(S_ + kc * 512, 512), A(S_ + kc * 512, 512),
                                       PBK[j].ap)
            ins.then_inc(semV, 1)
        sV(fv); cV.add(1)
        for j in range(4):
            PBK[j].rel(semV, cV.v)
        if hb == 1:
            s_ready = cV.v

    def main_G(i, hb):
        nonlocal gt_last
        sl = i % 3
        kt = arena[:, KVT + sl * 2052:KVT + sl * 2052 + 1024]
        def ft(t, _hb=hb, _u=u_t[i], _kt=kt):
            t.wait_ge(semV, _u)
            for j in range(4):
                idx = _hb * 4 + j
                cv, nh = idx // 2, idx % 2
                PBK[j].acq(t)
                ins = t.matmul(PBK[j].ap, A(U2B + cv * 128, 128),
                               _kt[:, nh * 512:(nh + 1) * 512], start=True, stop=True)
            ins.then_inc(semT, 1)
        sT(ft); cT.add(1)
        mmt = cT.v
        def fv(v, _t=mmt, _hb=hb, _first=(i == 0)):
            v.wait_ge(semT, _t)
            v.drain()
            for j in range(4):
                idx = _hb * 4 + j
                cv, nh = idx // 2, idx % 2
                dstp = A(GT_ + cv * 1024 + nh * 512, 512)
                if _first:
                    ins = v.tensor_copy(dstp, PBK[j].ap)
                else:
                    ins = v.tensor_add(dstp, dstp, PBK[j].ap)
            ins.then_inc(semV, 1)
        sV(fv); cV.add(1)
        for j in range(4):
            PBK[j].rel(semV, cV.v)
        if hb == 1:
            gt_last = cV.v
        kv_rel[i] = cT.v

    for j in range(2):
        prep_load(j)
        prep_compute1(j)
        for lev in range(6):
            pc2a_level(j, lev)
        for lev in range(7):
            pc2b_level(j, lev)
    for i in range(NCHUNK):
        if i + 2 < NCHUNK:
            prep_load(i + 2)
        main_kS(i)
        if i + 2 < NCHUNK:
            prep_compute1(i + 2)
        main_u(i)
        if i + 1 >= 2 and i + 1 < NCHUNK:
            for lev in range(4):
                pc2b_level(i + 1, lev)
        main_S(i, 0)
        if i + 1 >= 2 and i + 1 < NCHUNK:
            for lev in range(4, 7):
                pc2b_level(i + 1, lev)
        main_S(i, 1)
        if i + 2 < NCHUNK:
            pc2a_level(i + 2, 0)
            pc2a_level(i + 2, 1)
        main_G(i, 0)
        if i + 2 < NCHUNK:
            pc2a_level(i + 2, 2)
            pc2a_level(i + 2, 3)
        main_G(i, 1)
        if i + 2 < NCHUNK:
            pc2a_level(i + 2, 4)
            pc2a_level(i + 2, 5)
    def fg(g, _v=gt_last):
        g.wait_ge(semV, _v)
        for cv4 in range(4):
            gD(g, gtin[cv4 * 128:(cv4 + 1) * 128, :], A(GT_ + cv4 * 1024, 1024))
        g.wait_ge(semG, cG.v)
        g.collective_compute("AllGather", ALU.bypass,
                             replica_groups=[[0, 1], [2, 3], [4, 5], [6, 7]],
                             ins=[gtin], outs=[gtout]).then_inc(semC, 1)
    sG(fg); cC.add(1)
    gt_done = cC.v
    barrier()

    if upto == "solve":
        def fg(g):
            g.wait_ge(semC, gt_done)
            for tt in range(8):
                gD(g, yout[tt * 128:(tt + 1) * 128, :], gtout[tt * 128:(tt + 1) * 128, :])
            g.wait_ge(semG, cG.v)
        sG(fg)
        stack.close()
        return nc

    GTF = 0
    M2_ = 8192
    QT_ = 16384
    ZT_ = 18432
    AOE = 20480
    def fg(g, _c=gt_done):
        g.wait_ge(semC, _c)
        for cv in range(KCH):
            gD(g, A(GTF + cv * 1024, 1024), gtout[cv * 128:(cv + 1) * 128, :])
    sG(fg)
    gtf_ld = cG.v
    for ck in range(KCH):
        for nh in range(2):
            bank = PBK[ck % 2]
            def ft(t, _ck=ck, _nh=nh, _b=bank, _g=gtf_ld, _w=ow_loaded):
                t.wait_ge(semG, _g)
                t.wait_ge(semW, _w)
                _b.acq(t)
                for cv in range(KCH):
                    ins = t.matmul(_b.ap,
                                   arena[:, GTF + cv * 1024 + _ck * 128:GTF + cv * 1024 + (_ck + 1) * 128],
                                   arena[:, OWPF + cv * 1024 + _nh * 512:OWPF + cv * 1024 + (_nh + 1) * 512],
                                   start=(cv == 0), stop=(cv == KCH - 1))
                ins.then_inc(semT, 1)
            sT(ft); cT.add(1)
            mmt = cT.v
            def fv(v, _t=mmt, _ck=ck, _nh=nh, _b=bank):
                v.wait_ge(semT, _t)
                ins = v.tensor_copy(arena[:, M2_ + _ck * 1024 + _nh * 512:M2_ + _ck * 1024 + (_nh + 1) * 512],
                                    _b.ap)
                ins.then_inc(semV, 1)
            sV(fv); cV.add(1)
            bank.rel(semV, cV.v)
    m2_ready = cV.v
    qrel = {}
    for tt in range(TILES):
        qsl = QT_ + (tt % 2) * 1024
        zsl = ZT_ + (tt % 2) * 1024
        rel = qrel.get(tt - 2, 0)
        def fg(g, _tt=tt, _q=qsl, _z=zsl, _r=rel):
            if _r:
                g.wait_ge(semT, _r)
            gD(g, arena[:, _q:_q + 1024].rearrange("p (c f) -> p c f", c=8),
               qD[_tt])
            gD(g, A(_z, 1024), zD[_tt * 128:(_tt + 1) * 128, :])
        sG(fg)
        ld = cG.v
        aoe_done = 0
        for nh in range(2):
            bank = PBK[2 + nh]
            def ft(t, _q=qsl, _nh=nh, _b=bank, _ld=ld, _m2=m2_ready):
                t.wait_ge(semG, _ld)
                t.wait_ge(semV, _m2)
                _b.acq(t)
                for ck in range(KCH):
                    ins = t.matmul(_b.ap,
                                   arena[:, _q + ck * 128:_q + (ck + 1) * 128],
                                   arena[:, M2_ + ck * 1024 + _nh * 512:M2_ + ck * 1024 + (_nh + 1) * 512],
                                   start=(ck == 0), stop=(ck == KCH - 1))
                ins.then_inc(semT, 1)
            sT(ft); cT.add(1)
            if nh == 1:
                qrel[tt] = cT.v
            mmt = cT.v
            def fv(v, _t=mmt, _nh=nh, _b=bank, _z=zsl):
                v.wait_ge(semT, _t)
                v.drain()
                v.tensor_add(A(AOE + _nh * 512, 512), _b.ap, obb[:, _nh * 512:(_nh + 1) * 512])
                v.drain()
                ins = v.tensor_add(A(AOE + _nh * 512, 512), A(AOE + _nh * 512, 512),
                                   arena[:, _z + _nh * 512:_z + (_nh + 1) * 512])
                ins.then_inc(semV, 1)
            sV(fv); cV.add(1)
            bank.rel(semV, cV.v)
        z1r = cV.v
        def fg(g, _v=z1r, _tt=tt):
            g.wait_ge(semV, _v)
            gD(g, z1D[_tt * 128:(_tt + 1) * 128, :], A(AOE, 1024))
        sG(fg)
        z1s = cG.v
        def fv(v, _g=z1s):
            v.wait_ge(semG, _g)
        sV(fv)
    barrier()

    if upto == "attnout":
        def fg(g):
            for tt in range(TILES):
                gD(g, yout[tt * 128:(tt + 1) * 128, :], z1D[tt * 128:(tt + 1) * 128, :])
            g.wait_ge(semG, cG.v)
        sG(fg)
        stack.close()
        return nc

    W1H = 0
    W1L = 4096
    W2H = 8192
    W2L = 12288
    TFH = 16384
    TFL = 18432
    H1H = 20480
    H1L = 22528
    MZN2 = [24576, 28672]
    Z1B = [32768, 36864]
    XB_ = 40960
    VSCR = 45056
    wsplit = 0
    for j in range(8):
        for (src, hio, loo) in ((w1p_in, W1H, W1L), (w2p_in, W2H, W2L)):
            def fy(yq, _j=j, _src=src, _v=wsplit):
                if _v:
                    yq.wait_ge(semV, _v)
                yD(yq, A(VSCR, 1024), _src[:, _j * 1024:(_j + 1) * 1024])
            sY(fy)
            wt_ = cW.v
            def fv(v, _w=wt_, _j=j, _h=hio, _l=loo):
                v.wait_ge(semW, _w)
                v.drain()
                v.tensor_copy(AB(_h, _j * 1024, 1024), A(VSCR, 1024))
                v.drain()
                ins = v.tensor_tensor(AB(_l, _j * 1024, 1024), A(VSCR, 1024),
                                      AB(_h, _j * 1024, 1024), ALU.subtract)
                ins.then_inc(semV, 1)
            sV(fv); cV.add(1)
            wsplit = cV.v
    w12 = wsplit
    mlp2_pairs = [(PBK[2], PBK[3]), (PBK[0], PBK[1])]
    h1_free = 0
    xb_free = 0
    z1_free = [0, 0]
    for seg in range(4):
        zb = Z1B[seg % 2]
        mz = MZN2[seg % 2]
        def fg(g, _s=seg, _z=zb, _zf=z1_free[seg % 2]):
            if _zf:
                g.wait_ge(semV, _zf)
            gD(g, arena[:, _z:_z + 4096].rearrange("p (t c) -> p t c", t=4),
               z1D[_s * 512:(_s + 1) * 512, :].rearrange("(t p) c -> p t c", p=128))
        sG(fg)
        z1ld = cG.v
        tfv = []
        for tl in range(4):
            nd = ln_tile(A(zb + tl * 1024, 1024), A(mz + tl * 1024, 1024), VSCR,
                         wait_g=z1ld)
            transpose_tile(A(mz + tl * 1024, 1024), 0, 512, tl, wait_v=nd,
                           pair=(TFH, TFL, 512), affine=(mng, mnb, VSCR))
            tfv.append(cV.v)
        tf_done = cV.v
        for hc in range(KCH):
            bank = PBK[hc % 2]
            def ft(t, _hc=hc, _b=bank, _w=w12, _tf=tf_done, _hf=h1_free):
                t.wait_ge(semV, _tf)
                t.wait_ge(semV, _w)
                if _hf:
                    t.wait_ge(semT, _hf)
                _b.acq(t)
                n = 0
                for kc in range(KCH):
                    w1h = AB(W1H, kc * 1024 + _hc * 128, 128)
                    w1l = AB(W1L, kc * 1024 + _hc * 128, 128)
                    th = AB(TFH, kc * 512, 512)
                    tl_ = AB(TFL, kc * 512, 512)
                    for lhs, rhs in ((w1h, th), (w1h, tl_), (w1l, th)):
                        ins = t.matmul(_b.ap, lhs, rhs,
                                       start=(n == 0), stop=(n == 3 * KCH - 1))
                        n += 1
                ins.then_inc(semT, 1)
            sT(ft); cT.add(1)
            mmt = cT.v
            def fv(v, _t=mmt, _hc=hc, _b=bank):
                v.wait_ge(semT, _t)
                v.drain()
                v.tensor_scalar(A(VSCR, 512), _b.ap, b1c[:, _hc:_hc + 1], 0.0,
                                ALU.add, ALU.max)
                v.drain()
                v.tensor_copy(AB(H1H, _hc * 512, 512), A(VSCR, 512))
                v.drain()
                ins = v.tensor_tensor(AB(H1L, _hc * 512, 512), A(VSCR, 512),
                                      AB(H1H, _hc * 512, 512), ALU.subtract)
                ins.then_inc(semV, 1)
            sV(fv); cV.add(1)
            bank.rel(semV, cV.v)
        h1_ready = cV.v
        def fg(g, _s=seg, _x=xb_free):
            if _x:
                g.wait_ge(semV, _x)
            gD(g, arena[:, XB_:XB_ + 4096].rearrange("p (t c) -> p t c", t=4),
               xB[_s * 512:(_s + 1) * 512, :].rearrange("(t p) c -> p t c", p=128))
        sG(fg)
        xbld = cG.v
        for tl in range(4):
            bks = mlp2_pairs[tl % 2]
            def ft(t, _tl=tl, _b=bks, _h=h1_ready):
                t.wait_ge(semV, _h)
                _b[0].acq(t)
                _b[1].acq(t)
                n = [0, 0]
                for hc in range(KCH):
                    hh = AB(H1H, hc * 512 + _tl * 128, 128)
                    hl = AB(H1L, hc * 512 + _tl * 128, 128)
                    wh = [AB(W2H, hc * 1024 + nn * 512, 512) for nn in range(2)]
                    wl = [AB(W2L, hc * 1024 + nn * 512, 512) for nn in range(2)]
                    for lhs, which in ((hh, "h"), (hl, "l")):
                        for nn in range(2):
                            movs = (wh[nn], wl[nn]) if which == "h" else (wh[nn],)
                            for rhs in movs:
                                ins = t.matmul(_b[nn].ap, lhs, rhs,
                                               start=(n[nn] == 0),
                                               stop=(n[nn] == 3 * KCH - 1))
                                n[nn] += 1
                ins.then_inc(semT, 1)
            sT(ft); cT.add(1)
            if tl == 3:
                h1_free = cT.v
            mmt = cT.v
            def fv(v, _t=mmt, _b=bks, _tl=tl, _z=zb, _mz=mz, _xl=xbld):
                v.wait_ge(semT, _t)
                v.wait_ge(semG, _xl)
                for nn in range(2):
                    v.drain()
                    d = _mz + _tl * 1024 + nn * 512
                    v.tensor_add(A(d, 512), _b[nn].ap, b2b[:, nn * 512:(nn + 1) * 512])
                    v.drain()
                    v.tensor_add(A(d, 512), A(d, 512),
                                 arena[:, _z + _tl * 1024 + nn * 512:_z + _tl * 1024 + (nn + 1) * 512])
                    v.drain()
                    ins = v.tensor_add(A(d, 512), A(d, 512),
                                       arena[:, XB_ + _tl * 1024 + nn * 512:XB_ + _tl * 1024 + (nn + 1) * 512])
                ins.then_inc(semV, 1)
            sV(fv); cV.add(1)
            bks[0].rel(semV, cV.v)
            bks[1].rel(semV, cV.v)
            x2r = cV.v
            def fg(g, _v=x2r, _s=seg, _tl=tl, _mz=mz):
                g.wait_ge(semV, _v)
                gD(g, xA[_s * 512 + _tl * 128:_s * 512 + (_tl + 1) * 128, :],
                   A(_mz + _tl * 1024, 1024))
            sG(fg)
        xb_free = cV.v
        z1_free[seg % 2] = cV.v
    barrier()

    if upto == "mlp":
        def fg(g):
            for tt in range(TILES):
                gD(g, yout[tt * 128:(tt + 1) * 128, :], xA[tt * 128:(tt + 1) * 128, :])
            g.wait_ge(semG, cG.v)
        sG(fg)
        stack.close()
        return nc

    ffn_phase(xA, yout, arin2, arout2)

    def fg(g):
        g.wait_ge(semG, cG.v)
    sG(fg)
    stack.close()
    return nc


def _pack_weights(inp):
    w = {}
    gup = np.asarray(inp["L0_ffn_gate_up_proj"], np.float32)
    g = gup.reshape(E, 16, 128, 8, 128)
    g = np.ascontiguousarray(g.transpose(0, 1, 4, 3, 2))
    w["gup_pk"] = g.reshape(E, 16, 128, 8 * 128)
    dwn = np.asarray(inp["L0_ffn_down_proj"], np.float32)
    d = dwn.transpose(0, 2, 1)
    d = d.reshape(E, 8, 128, 1024)
    w["down_pk"] = np.ascontiguousarray(d.transpose(0, 2, 1, 3)).reshape(E, 128, 8 * 1024)
    gw = np.asarray(inp["L0_ffn_gate_weight"], np.float32)
    w["gw_t128"] = np.ascontiguousarray(
        gw.T.reshape(8, 128, 8).transpose(1, 0, 2)).reshape(128, 64)

    def packw(m):
        m = np.asarray(m, np.float32)
        n = m.shape[1]
        return np.ascontiguousarray(m.reshape(8, 128, n).transpose(1, 0, 2)).reshape(128, 8 * n)
    w["gatesw_p"] = packw(inp["L0_attn_gates_w"])
    w["outw_p"] = packw(inp["L0_attn_out_w"])
    w["w1_p"] = packw(inp["L0_attn_w1"])
    w["w2_p"] = packw(inp["L0_attn_w2"])
    gb = np.asarray(inp["L0_attn_gates_b"], np.float32)
    w["gates_b"] = gb.reshape(1, 3072)
    w["qbc"] = np.ascontiguousarray(gb[0:1024].reshape(8, 128).T)
    w["out_b"] = np.asarray(inp["L0_attn_out_b"], np.float32).reshape(1, 1024)
    w["b2"] = np.asarray(inp["L0_attn_b2"], np.float32).reshape(1, 1024)
    w["lr_b"] = np.asarray(inp["L0_attn_lr_b"], np.float32).reshape(1, 1)
    w["lrw_c"] = np.ascontiguousarray(
        np.asarray(inp["L0_attn_lr_w"], np.float32).reshape(8, 128).T)
    w["b1c"] = np.ascontiguousarray(np.asarray(inp["L0_attn_b1"], np.float32).reshape(8, 128).T)
    for nm, key in [("dn_g_c", "L0_attn_dn_g"), ("dn_b_c", "L0_attn_dn_b"),
                    ("mn_g_c", "L0_attn_mn_g"), ("mn_b_c", "L0_attn_mn_b")]:
        w[nm] = np.ascontiguousarray(np.asarray(inp[key], np.float32).reshape(8, 128).T)
    w["simm"] = np.asarray(inp["L0_ffn_similarity_matrix"], np.float32)
    w["ident128"] = np.eye(128, dtype=np.float32)
    w["tril128"] = np.tril(np.ones((128, 128), np.float32), -1)
    w["iota8b"] = np.tile(np.arange(8, dtype=np.float32), (128, 1))
    w["iotaBb"] = w["iota8b"] + BIG
    w["iotap"] = np.arange(128, dtype=np.float32).reshape(128, 1)
    w["epsc"] = np.full((128, 1), EPS, np.float32)
    w["ones1x128"] = np.ones((1, 128), np.float32)
    w["onec"] = np.ones((128, 1), np.float32)
    w["ident8"] = np.eye(8, dtype=np.float32)
    return w


def kernel(**inputs):
    x = np.asarray(inputs["L0_attn_x"], np.float32)
    w = _pack_weights(inputs)
    nc = build_nc("full")
    in_maps = []
    for c in range(NCORES):
        seq, half = c // 2, c % 2
        im = dict(w)
        im["xshard"] = np.ascontiguousarray(x[seq, half * TCORE:(half + 1) * TCORE, :])
        im["parity"] = np.full((128, 1), float(half), np.float32)
        in_maps.append(im)
    res = run_bass_kernel_spmd(nc, in_maps, core_ids=list(range(NCORES)))
    out = np.empty((4, TSEQ, C), np.float32)
    for c in range(NCORES):
        seq, half = c // 2, c % 2
        out[seq, half * TCORE:(half + 1) * TCORE, :] = res.results[c]["y"]
    return out


# version 11
# speedup vs baseline: 1.1309x; 1.1309x over previous
import numpy as np
import concourse.bass as bass
import concourse.mybir as mybir
from concourse.bass_utils import run_bass_kernel_spmd

F32 = mybir.dt.float32
BF16 = mybir.dt.bfloat16
AF = mybir.ActivationFunctionType
ALU = mybir.AluOpType
AX = mybir.AxisListType

NCORES = 8
TCORE = 2048
C = 1024
KCH = 8
TILES = 16
E = 8
TSEQ = 4096
NCHUNK = 32
BIG = 1.0e6
EPS = 1e-5


class Ctr:
    __slots__ = ("v",)
    def __init__(self):
        self.v = 0
    def add(self, n=1):
        self.v += n
        return self.v


class PB:
    __slots__ = ("ap", "sem", "val")
    def __init__(self, ap):
        self.ap = ap
        self.sem = None
        self.val = 0
    def acq(self, t):
        if self.sem is not None and self.val:
            t.wait_ge(self.sem, self.val)
    def rel(self, sem, val):
        self.sem = sem
        self.val = val


def build_nc(upto="full"):
    nc = bass.Bass("TRN2", target_bir_lowering=False, debug=False)

    dt = lambda n, s: nc.dram_tensor(n, s, F32, kind="ExternalInput").ap()
    xin = dt("xshard", [TCORE, C])
    parity_in = dt("parity", [128, 1])
    ident_in = dt("ident128", [128, 128])
    tril_in = dt("tril128", [128, 128])
    iota8_in = dt("iota8b", [128, 8])
    iotaB_in = dt("iotaBb", [128, 8])
    iotap_in = dt("iotap", [128, 1])
    eps_in = dt("epsc", [128, 1])
    onec_in = dt("onec", [128, 1])
    ones_in = dt("ones1x128", [1, 128])
    id8_in = dt("ident8", [8, 8])
    gw_in = dt("gw_t128", [128, 64])
    gwp_in = dt("gatesw_p", [128, KCH * 3072])
    gb_in = dt("gates_b", [1, 3072])
    qbc_in = dt("qbc", [128, 8])
    lrw_in = dt("lrw_c", [128, 8])
    lrb_in = dt("lr_b", [1, 1])
    owp_in = dt("outw_p", [128, KCH * 1024])
    ob_in = dt("out_b", [1, 1024])
    w1p_in = dt("w1_p", [128, KCH * 1024])
    b1c_in = dt("b1c", [128, 8])
    w2p_in = dt("w2_p", [128, KCH * 1024])
    b2_in = dt("b2", [1, 1024])
    dng_in = dt("dn_g_c", [128, 8])
    dnb_in = dt("dn_b_c", [128, 8])
    mng_in = dt("mn_g_c", [128, 8])
    mnb_in = dt("mn_b_c", [128, 8])
    sim_in = dt("simm", [8, 8])
    gup_in = dt("gup_pk", [E, 16, 128, KCH * 128])
    dwn_in = dt("down_pk", [E, 128, 8 * 1024])

    di = lambda n, s: nc.dram_tensor(n, s, F32, kind="Internal").ap()
    xA = di("xA", [TCORE, C])
    xB = di("xB", [TCORE, C])
    zD = di("zD", [TCORE, C])
    z1D = di("z1D", [TCORE, C])
    qD = di("qD", [16, 128, KCH, 128])
    kvin = di("kvin", [TCORE, 2049])
    kvoutA = di("kvoutA", [TCORE, 2049])
    kvoutB = di("kvoutB", [TCORE, 2049])
    gtin = di("gtin", [512, C])
    gtout = di("gtout", [C, C])
    arin1 = di("arin1", [1, 8])
    arout1 = di("arout1", [1, 8])
    arin2 = di("arin2", [1, 8])
    arout2 = di("arout2", [1, 8])

    yout = nc.dram_tensor("y", [TCORE, C], F32, kind="ExternalOutput").ap()

    sb = lambda n, w: nc.alloc_sbuf_tensor(n, [128, w], F32).ap()
    ident = sb("ident", 128)
    tril = sb("tril", 128)
    iota8 = sb("iota8", 8)
    iotaB = sb("iotaB", 8)
    iotap = sb("iotap_s", 1)
    epsc = sb("epsc_s", 1)
    onec = sb("onec_s", 1)
    par = sb("par_s", 1)
    onesr = nc.alloc_sbuf_tensor("onesr", [1, 128], F32).ap()
    id8 = nc.alloc_sbuf_tensor("id8s", [8, 8], F32).ap()
    simm = nc.alloc_sbuf_tensor("simms", [8, 8], F32).ap()
    gwt = sb("gwt", 64)
    gwtH = nc.alloc_sbuf_tensor("gwtH", [128, 64], BF16).ap()
    gwtL = nc.alloc_sbuf_tensor("gwtL", [128, 64], BF16).ap()
    qbc = sb("qbc_s", 8)
    lrw = sb("lrw_s", 8)
    b1c = sb("b1c_s", 8)
    dng = sb("dng", 8)
    dnb = sb("dnb", 8)
    mng = sb("mng", 8)
    mnb = sb("mnb", 8)
    gbb = sb("gbb", 2048)
    obb = sb("obb", 1024)
    b2b = sb("b2b", 1024)
    lrbb = sb("lrbb", 1)
    rt = sb("rt", 64)
    comb = sb("combs", 128)
    mapb = sb("mapb", 8)
    cntacc = sb("cntacc", 8)
    srow = nc.alloc_sbuf_tensor("srow", [1, 8], F32).ap()
    scol = nc.alloc_sbuf_tensor("scol", [8, 1], F32).ap()
    s8a = nc.alloc_sbuf_tensor("s8a", [8, 8], F32).ap()
    s8b = nc.alloc_sbuf_tensor("s8b", [8, 8], F32).ap()
    s8c = nc.alloc_sbuf_tensor("s8c", [8, 8], F32).ap()
    c81 = nc.alloc_sbuf_tensor("c81", [8, 1], F32).ap()
    c82 = nc.alloc_sbuf_tensor("c82", [8, 1], F32).ap()
    c83 = nc.alloc_sbuf_tensor("c83", [8, 1], F32).ap()
    bnos = sb("bnos", 12)
    bnag = sb("bnag", 2)
    st1 = sb("st1", 1)
    st2 = sb("st2", 1)
    st3 = sb("st3", 1)
    st4 = sb("st4", 1)

    ARENA_W = 46080
    arena = nc.alloc_sbuf_tensor("arena", [128, ARENA_W], F32).ap()
    def A(off, w):
        assert 0 <= off and off + w <= ARENA_W, (off, w)
        return arena[:, off:off + w]

    def AB(f32_base, bidx, bw):
        assert bidx % 2 == 0 and bw % 2 == 0
        off = f32_base + bidx // 2
        assert off + bw // 2 <= ARENA_W, (f32_base, bidx, bw)
        return arena[:, off:off + bw // 2].bitcast(BF16)

    pbt = [nc.alloc_psum_tensor(f"pb{i}", [128, 512], F32).ap() for i in range(8)]
    PBK = [PB(pbt[i]) for i in range(8)]

    import contextlib
    stack = contextlib.ExitStack()
    semW = stack.enter_context(nc.semaphore("semW")); cW = Ctr()
    semG = stack.enter_context(nc.semaphore("semG")); cG = Ctr()
    semT = stack.enter_context(nc.semaphore("semT")); cT = Ctr()
    semV = stack.enter_context(nc.semaphore("semV")); cV = Ctr()
    semS = stack.enter_context(nc.semaphore("semS")); cS = Ctr()
    semC = stack.enter_context(nc.semaphore("semC")); cC = Ctr()
    semB = stack.enter_context(nc.semaphore("semB")); cB = Ctr()
    semP = stack.enter_context(nc.semaphore("semP")); cP = Ctr()
    blk = stack.enter_context(nc.Block())

    def sT(f): blk.tensor(f)
    def sV(f): blk.vector(f)
    def sS(f): blk.scalar(f)
    def sG(f): blk.gpsimd(f)
    def sY(f): blk.sync(f)

    def barrier():
        tgt = cB.add(5)
        for sec in (sT, sV, sS, sG, sY):
            def f(e, _t=tgt):
                e.drain()
                e.nop().then_inc(semB, 1)
                e.wait_ge(semB, _t)
            sec(f)

    def gD(g, out, in_, slow=False):
        g.wait_ge(semG, cG.v)
        g.dma_start(out=out, in_=in_, allow_slow_non_contiguous=slow).then_inc(semG, 16)
        cG.add(16)

    def yD(y, out, in_):
        y.wait_ge(semW, cW.v)
        y.dma_start(out=out, in_=in_).then_inc(semW, 16)
        cW.add(16)

    def f(y):
        for dst, src in [(ident, ident_in), (tril, tril_in), (iota8, iota8_in),
                         (iotaB, iotaB_in), (iotap, iotap_in), (epsc, eps_in),
                         (par, parity_in), (onesr, ones_in), (id8, id8_in), (onec, onec_in),
                         (simm, sim_in), (gwt, gw_in), (qbc, qbc_in),
                         (lrw, lrw_in), (b1c, b1c_in), (dng, dng_in),
                         (dnb, dnb_in), (mng, mng_in), (mnb, mnb_in)]:
            yD(y, dst, src)
        yD(y, arena[0:1, 0:3072], gb_in)
        yD(y, arena[0:1, 3072:4096], ob_in)
        yD(y, arena[0:1, 4096:5120], b2_in)
        yD(y, arena[0:1, 5120:5121], lrb_in)
    sY(f)

    def f(t):
        t.wait_ge(semW, cW.v)
        for j in range(4):
            t.matmul(PBK[j].ap[:, 0:512], onesr,
                     arena[0:1, 1024 + j * 512:1024 + (j + 1) * 512], start=True, stop=True)
        for j in range(2):
            t.matmul(PBK[4 + j].ap[:, 0:512], onesr,
                     arena[0:1, 3072 + j * 512:3072 + (j + 1) * 512], start=True, stop=True)
        for j in range(2):
            ins = t.matmul(PBK[6 + j].ap[:, 0:512], onesr,
                           arena[0:1, 4096 + j * 512:4096 + (j + 1) * 512], start=True, stop=True)
        ins.then_inc(semT, 1)
    sT(f); cT.add(1)
    def f(v, _t=cT.v):
        v.wait_ge(semT, _t)
        for j in range(4):
            v.tensor_copy(gbb[:, j * 512:(j + 1) * 512], PBK[j].ap[:, 0:512])
        for j in range(2):
            v.tensor_copy(obb[:, j * 512:(j + 1) * 512], PBK[4 + j].ap[:, 0:512])
        v.tensor_copy(b2b[:, 0:512], PBK[6].ap[:, 0:512])
        ins = v.tensor_copy(b2b[:, 512:1024], PBK[7].ap[:, 0:512])
        ins.then_inc(semV, 1)
    sV(f); cV.add(1)
    for b in PBK:
        b.rel(semV, cV.v)
    def f(t, _v=cV.v):
        t.wait_ge(semV, _v)
        t.matmul(PBK[0].ap[:, 0:1], onesr, arena[0:1, 5120:5121],
                 start=True, stop=True).then_inc(semT, 1)
    sT(f); cT.add(1)
    def f(v, _t=cT.v):
        v.wait_ge(semT, _t)
        v.tensor_copy(lrbb, PBK[0].ap[:, 0:1]).then_inc(semV, 1)
    sV(f); cV.add(1)
    PBK[0].rel(semV, cV.v)
    def f(v):
        v.drain()
        v.tensor_copy(gwtH, gwt)
        v.drain()
        v.tensor_tensor(gwtL, gwt, gwtH, ALU.subtract).then_inc(semV, 1)
    sV(f); cV.add(1)
    barrier()

    def ln_tile(x_ap, out_ap, sq_off, wait_g=0):
        def fv(v, _g=wait_g, _s=cS.v):
            if _g:
                v.wait_ge(semG, _g)
            v.wait_ge(semS, _s)
            v.drain()
            v.bn_stats(bnos[:, 0:6], x_ap[:, 0:512])
            v.drain()
            v.bn_stats(bnos[:, 6:12], x_ap[:, 512:1024])
            v.drain()
            v.bn_aggr(bnag, bnos).then_inc(semV, 1)
        sV(fv); cV.add(1)
        def fs(s, _v=cV.v):
            s.wait_ge(semV, _v)
            s.activation(st4, bnag[:, 1:2], AF.Sqrt, bias=epsc, scale=1.0).then_inc(semS, 1)
        sS(fs); cS.add(1)
        def fv(v, _s=cS.v):
            v.wait_ge(semS, _s)
            v.reciprocal(st4, st4)
            v.drain()
            v.tensor_scalar(out_ap, x_ap, bnag[:, 0:1], st4,
                            ALU.subtract, ALU.mult).then_inc(semV, 1)
        sV(fv); cV.add(1)
        return cV.v

    def transpose_tile(src_ap, dst_base, dst_stride, tile_idx, affine=None,
                       wait_v=0, banks=(5, 6), pair=None):
        outs = []
        for gi, bi in enumerate(banks):
            bank = PBK[bi]
            def ft(t, _g=gi, _b=bank, _w=wait_v):
                if _w:
                    t.wait_ge(semV, _w)
                _b.acq(t)
                for j in range(4):
                    kc = _g * 4 + j
                    ins = t.transpose(_b.ap[:, j * 128:(j + 1) * 128],
                                      src_ap[:, kc * 128:(kc + 1) * 128], ident)
                ins.then_inc(semT, 1)
            sT(ft); cT.add(1)
            ttgt = cT.v
            if pair is not None:
                hiB, loB, bstride = pair
                def fe(v, _g=gi, _b=bank, _t=ttgt):
                    v.wait_ge(semT, _t)
                    for j in range(4):
                        kc = _g * 4 + j
                        bidx = kc * bstride + tile_idx * 128
                        hi = AB(hiB, bidx, 128)
                        lo = AB(loB, bidx, 128)
                        if affine is None:
                            v.tensor_copy(hi, _b.ap[:, j * 128:(j + 1) * 128])
                            v.drain()
                            ins = v.tensor_tensor(lo, _b.ap[:, j * 128:(j + 1) * 128],
                                                  hi, ALU.subtract)
                            v.drain()
                        else:
                            g_ap, b_ap, scr = affine
                            v.tensor_scalar(A(scr, 128), _b.ap[:, j * 128:(j + 1) * 128],
                                            g_ap[:, kc:kc + 1], b_ap[:, kc:kc + 1],
                                            ALU.mult, ALU.add)
                            v.drain()
                            v.tensor_copy(hi, A(scr, 128))
                            v.drain()
                            ins = v.tensor_tensor(lo, A(scr, 128), hi, ALU.subtract)
                            v.drain()
                    ins.then_inc(semV, 1)
                sV(fe); cV.add(1)
                bank.rel(semV, cV.v)
                outs.append(cV.v)
                continue
            use_v = (gi == 0)
            def fe(e, _g=gi, _b=bank, _t=ttgt, _use_v=use_v):
                e.wait_ge(semT, _t)
                for j in range(4):
                    kc = _g * 4 + j
                    d0 = dst_base + kc * dst_stride + tile_idx * 128
                    dst = A(d0, 128)
                    if affine is None:
                        if _use_v:
                            ins = e.tensor_copy(dst, _b.ap[:, j * 128:(j + 1) * 128])
                        else:
                            ins = e.copy(dst, _b.ap[:, j * 128:(j + 1) * 128])
                    else:
                        g_ap, b_ap = affine
                        if _use_v:
                            ins = e.tensor_scalar(dst, _b.ap[:, j * 128:(j + 1) * 128],
                                                  g_ap[:, kc:kc + 1], b_ap[:, kc:kc + 1],
                                                  ALU.mult, ALU.add)
                        else:
                            ins = e.activation(dst, _b.ap[:, j * 128:(j + 1) * 128], AF.Identity,
                                               bias=b_ap[:, kc:kc + 1], scale=g_ap[:, kc:kc + 1])
                ins.then_inc(semV if _use_v else semS, 1)
            if use_v:
                sV(fe); cV.add(1)
                bank.rel(semV, cV.v)
                outs.append(cV.v)
            else:
                sS(fe); cS.add(1)
                bank.rel(semS, cS.v)
                outs.append(cS.v)
        return outs[0], outs[1]

    ZNH = 0
    ZNL = 8192
    OUTH = 16384
    HMID = 24576
    DOWN = 32768
    GB2 = 40960
    WSTG = 43008
    WPR = [44032, 45056, 41984]
    DWH = 32768
    DWL = 36864
    HMH = 24576
    HML = 28672
    XTS = 43008
    PPX = 24576
    PPZ = 28672
    PPSQ = 30720

    def ffn_phase(src_ap, dst_ap, arin, arout):
        stage_free = [0]
        wslot_free = [0, 0]
        dw_free = 0
        down_free = 0

        def stage_load(src, hi_off, lo_off, width_f32, free_wait):
            def fy(yq, _w=stage_free[0], _g=cG.v):
                if _w:
                    yq.wait_ge(semV, _w)
                yq.wait_ge(semG, _g)
                yD(yq, A(WSTG, width_f32), src)
            sY(fy)
            wtgt = cW.v
            def fs(sx, _w=wtgt, _free=free_wait):
                sx.wait_ge(semW, _w)
                if _free:
                    sx.wait_ge(semT, _free)
                ins = sx.activation(AB(hi_off, 0, width_f32), A(WSTG, width_f32),
                                    AF.Identity, bias=0.0, scale=1.0)
                ins.then_inc(semS, 1)
            sS(fs); cS.add(1)
            cast_t = cS.v
            def fv(v, _c=cast_t):
                v.wait_ge(semS, _c)
                v.drain()
                ins = v.tensor_tensor(AB(lo_off, 0, width_f32), A(WSTG, width_f32),
                                      AB(hi_off, 0, width_f32), ALU.subtract)
                ins.then_inc(semV, 1)
            sV(fv); cV.add(1)
            stage_free[0] = cV.v
            return (cast_t, cV.v)

        gu_order = []
        for _h in range(2):
            for _e in range(E):
                for _mi, _m in enumerate([x for pr in zip(range(8), range(8, 16)) for x in pr]):
                    gu_order.append((_h, _e, _mi, _m))
        wsp_map = {}
        wslot_free3 = {0: 0, 1: 0, 2: 0}
        gptr = [0]

        def stage_next_gu():
            gidx = gptr[0]
            if gidx >= len(gu_order):
                return
            gptr[0] += 1
            _h, _e, _mi, _m = gu_order[gidx]
            slot = gidx % 3
            wsp_map[(_h, _e, _mi)] = (
                stage_load(gup_in[_e, _m], WPR[slot], WPR[slot] + 512, 1024,
                           wslot_free3[slot]), slot)

        stage_next_gu()
        stage_next_gu()

        def fv(v):
            v.memset(cntacc, 0.0).then_inc(semV, 1)
        sV(fv); cV.add(1)
        xfree = {}
        sig_last = [0]
        znv = [0, 0]
        for tt in range(TILES):
            slot = PPX + (tt % 2) * 2048
            wtgt = xfree.get(tt - 2, 0)
            def fg(g, _tt=tt, _s=slot, _w=wtgt):
                if _w:
                    g.wait_ge(semT, _w)
                gD(g, A(_s, 1024), src_ap[_tt * 128:(_tt + 1) * 128, :])
            sG(fg)
            ln_done = ln_tile(A(slot, 1024), A(PPZ, 1024), PPSQ, wait_g=cG.v)
            transpose_tile(A(PPZ, 1024), 0, 2048, tt, wait_v=ln_done,
                           pair=(ZNH, ZNL, 2048))
            lb = PBK[4 + 3 * (tt % 2)]
            ev_v = cV.v
            def ft(t, _tt=tt, _b=lb, _ev=ev_v):
                t.wait_ge(semV, _ev)
                _b.acq(t)
                n = 0
                for kc in range(KCH):
                    zh = AB(ZNH, kc * 2048 + _tt * 128, 128)
                    zl = AB(ZNL, kc * 2048 + _tt * 128, 128)
                    for lhs, rhs in ((zh, gwtH[:, kc * 8:(kc + 1) * 8]),
                                     (zh, gwtL[:, kc * 8:(kc + 1) * 8]),
                                     (zl, gwtH[:, kc * 8:(kc + 1) * 8])):
                        ins = t.matmul(_b.ap[:, 0:8], lhs, rhs,
                                       start=(n == 0), stop=(n == 3 * KCH - 1))
                        n += 1
                ins.then_inc(semT, 1)
            sT(ft); cT.add(1)
            xfree[tt] = cT.v
            ttgt = cT.v
            rtc = rt[:, tt * 4:tt * 4 + 4]
            def fv(v, _t=ttgt, _b=lb, _rtc=rtc):
                v.wait_ge(semT, _t)
                l = A(PPSQ, 8)
                v.tensor_copy(l, _b.ap[:, 0:8])
                v.drain()
                v.tensor_reduce(st1, l, AX.X, ALU.max)
                v.drain()
                ohr = A(PPSQ + 8, 8)
                v.tensor_scalar(ohr, l, st1, None, ALU.is_equal)
                v.drain()
                t1 = A(PPSQ + 16, 8)
                v.scalar_tensor_tensor(t1, ohr, -BIG, iotaB, ALU.mult, ALU.add)
                v.drain()
                v.tensor_reduce(_rtc[:, 0:1], t1, AX.X, ALU.min)
                v.drain()
                oh0 = A(PPSQ + 24, 8)
                v.tensor_scalar(oh0, iota8, _rtc[:, 0:1], None, ALU.is_equal)
                v.drain()
                v.tensor_add(cntacc, cntacc, oh0)
                v.drain()
                l2 = A(PPSQ + 32, 8)
                v.scalar_tensor_tensor(l2, oh0, -BIG, l, ALU.mult, ALU.add)
                v.drain()
                v.tensor_reduce(st2, l2, AX.X, ALU.max)
                v.drain()
                ohr2 = A(PPSQ + 40, 8)
                v.tensor_scalar(ohr2, l2, st2, None, ALU.is_equal)
                v.drain()
                t2 = A(PPSQ + 48, 8)
                v.scalar_tensor_tensor(t2, ohr2, -BIG, iotaB, ALU.mult, ALU.add)
                v.drain()
                v.tensor_reduce(_rtc[:, 1:2], t2, AX.X, ALU.min)
                v.drain()
                v.tensor_sub(st3, st1, st2).then_inc(semV, 1)
            sV(fv); cV.add(1)
            lb.rel(semV, cV.v)
            def fs(s, _v=cV.v, _rtc=rtc):
                s.wait_ge(semV, _v)
                s.activation(_rtc[:, 2:3], st3, AF.Sigmoid, bias=0.0, scale=1.0)
                ins = s.activation(_rtc[:, 3:4], st3, AF.Sigmoid, bias=0.0, scale=-1.0)
                ins.then_inc(semS, 1)
            sS(fs); cS.add(1)
            sig_last[0] = cS.v
            if tt == 7:
                znv[0] = cV.v
            if tt == 15:
                znv[1] = cV.v

        cnt_mm = PBK[4]
        def ft(t, _v=cV.v):
            t.wait_ge(semV, _v)
            cnt_mm.acq(t)
            t.matmul(cnt_mm.ap[0:1, 8:16], onec, cntacc, start=True, stop=True).then_inc(semT, 1)
        sT(ft); cT.add(1)
        def fv(v, _t=cT.v):
            v.wait_ge(semT, _t)
            v.tensor_copy(srow, cnt_mm.ap[0:1, 8:16]).then_inc(semV, 1)
        sV(fv); cV.add(1)
        cnt_mm.rel(semV, cV.v)
        def fg(g, _v=cV.v):
            g.wait_ge(semV, _v)
            gD(g, arin, srow)
            g.wait_ge(semG, cG.v)
            g.collective_compute("AllReduce", ALU.add,
                                 replica_groups=[list(range(NCORES))],
                                 ins=[arin], outs=[arout]).then_inc(semC, 1)
        sG(fg); cC.add(1)
        ar_done = cC.v
        def fg(g, _c=ar_done):
            g.wait_ge(semC, _c)
            gD(g, srow, arout)
            gD(g, scol, arout.rearrange("1 e -> e 1"), slow=True)
        sG(fg)
        def fv(v, _g=cG.v):
            v.wait_ge(semG, _g)
            v.tensor_scalar(srow, srow, 0.0, None, ALU.is_gt)
            ins = v.tensor_scalar(scol, scol, 0.0, None, ALU.is_gt)
            ins.then_inc(semV, 1)
        sV(fv); cV.add(1)
        def ft(t, _v=cV.v):
            t.wait_ge(semV, _v)
            cnt_mm.acq(t)
            t.matmul(cnt_mm.ap[0:8, 16:24], onesr[:, 0:8], srow,
                     start=True, stop=True).then_inc(semT, 1)
        sT(ft); cT.add(1)
        def fv(v, _t=cT.v):
            v.wait_ge(semT, _t)
            v.tensor_copy(s8a, cnt_mm.ap[0:8, 16:24])
            v.drain()
            v.tensor_mul(s8b, simm, s8a)
            v.drain()
            v.tensor_scalar(s8c, s8a, BIG, -BIG, ALU.mult, ALU.add)
            v.drain()
            v.tensor_add(s8b, s8b, s8c)
            v.drain()
            v.tensor_reduce(c81, s8b, AX.X, ALU.max)
            v.drain()
            v.tensor_scalar(s8c, s8b, c81, None, ALU.is_equal)
            v.drain()
            v.scalar_tensor_tensor(s8c, s8c, -BIG, iotaB[0:8, :], ALU.mult, ALU.add)
            v.drain()
            v.tensor_reduce(c82, s8c, AX.X, ALU.min)
            v.drain()
            v.tensor_scalar(c81, c81, 0.7, None, ALU.is_ge)
            v.drain()
            v.tensor_scalar(c83, scol, -1.0, 1.0, ALU.mult, ALU.add)
            v.drain()
            v.tensor_mul(c81, c81, c83)
            v.drain()
            v.tensor_mul(c82, c82, c81)
            v.drain()
            v.tensor_scalar(c83, c81, -1.0, 1.0, ALU.mult, ALU.add)
            v.drain()
            v.tensor_mul(c83, c83, iotap[0:8, :])
            v.drain()
            ins = v.tensor_add(c82, c82, c83)
            ins.then_inc(semV, 1)
        sV(fv); cV.add(1)
        cnt_mm.rel(semV, cV.v)
        def ft(t, _v=cV.v):
            t.wait_ge(semV, _v)
            cnt_mm.acq(t)
            t.matmul(cnt_mm.ap[0:1, 24:32], c82, id8, start=True, stop=True).then_inc(semT, 1)
        sT(ft); cT.add(1)
        def fv(v, _t=cT.v):
            v.wait_ge(semT, _t)
            v.tensor_copy(srow, cnt_mm.ap[0:1, 24:32]).then_inc(semV, 1)
        sV(fv); cV.add(1)
        cnt_mm.rel(semV, cV.v)
        def ft(t, _v=cV.v):
            t.wait_ge(semV, _v)
            cnt_mm.acq(t)
            t.matmul(cnt_mm.ap[:, 32:40], onesr, srow, start=True, stop=True).then_inc(semT, 1)
        sT(ft); cT.add(1)
        def fv(v, _t=cT.v, _sig=sig_last[0]):
            v.wait_ge(semT, _t)
            v.wait_ge(semS, _sig)
            v.tensor_copy(mapb, cnt_mm.ap[:, 32:40])
            v.drain()
            for tt2 in range(TILES):
                rtc = rt[:, tt2 * 4:tt2 * 4 + 4]
                oh = A(PPSQ, 8)
                t3 = A(PPSQ + 8, 8)
                v.tensor_scalar(oh, iota8, rtc[:, 0:1], None, ALU.is_equal)
                v.drain()
                v.tensor_mul(oh, oh, mapb)
                v.drain()
                v.tensor_reduce(st1, oh, AX.X, ALU.add)
                v.drain()
                v.tensor_scalar(oh, iota8, rtc[:, 1:2], None, ALU.is_equal)
                v.drain()
                v.tensor_mul(oh, oh, mapb)
                v.drain()
                v.tensor_reduce(st2, oh, AX.X, ALU.add)
                v.drain()
                v.tensor_scalar(oh, iota8, st1, rtc[:, 2:3], ALU.is_equal, ALU.mult)
                v.drain()
                v.tensor_scalar(t3, iota8, st2, rtc[:, 3:4], ALU.is_equal, ALU.mult)
                v.drain()
                ins = v.tensor_add(comb[:, tt2 * 8:(tt2 + 1) * 8], oh, t3)
                v.drain()
            ins.then_inc(semV, 1)
        sV(fv); cV.add(1)
        cnt_mm.rel(semV, cV.v)
        comb_ready = cV.v

        gslot_free = (semV, 0)
        mlist = [x for pr in zip(range(8), range(8, 16)) for x in pr]
        hmid_guard = comb_ready
        gu_pairs = [(PBK[0], PBK[1]), (PBK[4], PBK[7])]
        dn_pairs = [(PBK[2], PBK[3]), (PBK[0], PBK[1])]

        for half in range(2):
            for e in range(E):
                dn_loaded = {}
                g_done = 0
                hm_last = 0
                for mi, m in enumerate(mlist):
                    stage_next_gu()
                    if mi >= 8:
                        mf_d = mi - 8
                        dn_loaded[mf_d] = stage_load(
                            dwn_in[e][:, mf_d * 1024:(mf_d + 1) * 1024],
                            DWH + mf_d * 512, DWL + mf_d * 512, 1024, dw_free)
                    wsp, slot = wsp_map[(half, e, mi)]
                    bks = gu_pairs[mi % 2]
                    def ft(t, _s=slot, _w=wsp, _zn=znv[half], _b=bks, _half=half):
                        t.wait_ge(semS, _w[0])
                        t.wait_ge(semV, max(_w[1], _zn))
                        _b[0].acq(t)
                        _b[1].acq(t)
                        n = [0, 0]
                        for kc in range(KCH):
                            wh = AB(WPR[_s], kc * 128, 128)
                            wl = AB(WPR[_s] + 512, kc * 128, 128)
                            zh = [AB(ZNH, kc * 2048 + _half * 1024 + nn * 512, 512) for nn in range(2)]
                            zl = [AB(ZNL, kc * 2048 + _half * 1024 + nn * 512, 512) for nn in range(2)]
                            for lhs, which in ((wh, "h"), (wl, "l")):
                                for nn in range(2):
                                    movs = (zh[nn], zl[nn]) if which == "h" else (zh[nn],)
                                    for rhs in movs:
                                        ins = t.matmul(_b[nn].ap, lhs, rhs,
                                                       start=(n[nn] == 0),
                                                       stop=(n[nn] == 3 * KCH - 1))
                                        n[nn] += 1
                        ins.then_inc(semT, 1)
                    sT(ft); cT.add(1)
                    wslot_free3[slot] = cT.v
                    mmtgt = cT.v
                    if m < 8:
                        gw_ = gslot_free
                        def fs(s, _t=mmtgt, _b=bks, _gw=gw_):
                            s.wait_ge(semT, _t)
                            if _gw[1]:
                                s.wait_ge(_gw[0], _gw[1])
                            s.activation(A(GB2, 512), _b[0].ap, AF.Silu, bias=0.0, scale=1.0)
                            ins = s.activation(A(GB2 + 512, 512), _b[1].ap, AF.Silu,
                                               bias=0.0, scale=1.0)
                            ins.then_inc(semS, 1)
                        sS(fs); cS.add(1)
                        bks[0].rel(semS, cS.v)
                        bks[1].rel(semS, cS.v)
                        g_done = cS.v
                    else:
                        mf = m - 8
                        hb = PBK[3]
                        def fv(v, _mf=mf, _t=mmtgt, _b=bks, _gd=g_done, _hg=hmid_guard,
                               _s=hb):
                            v.wait_ge(semT, _t)
                            v.wait_ge(semS, _gd)
                            if _hg:
                                v.wait_ge(semV, _hg)
                            if _s.sem is not None and _s.val:
                                v.wait_ge(_s.sem, _s.val)
                            for nn in range(2):
                                v.drain()
                                v.tensor_mul(_s.ap, A(GB2 + nn * 512, 512), _b[nn].ap)
                                v.drain()
                                v.tensor_copy(AB(HMH, _mf * 1024 + nn * 512, 512), _s.ap)
                                v.drain()
                                ins = v.tensor_tensor(AB(HML, _mf * 1024 + nn * 512, 512),
                                                      _s.ap,
                                                      AB(HMH, _mf * 1024 + nn * 512, 512),
                                                      ALU.subtract)
                            ins.then_inc(semV, 1)
                        sV(fv); cV.add(1)
                        hb.rel(semV, cV.v)
                        bks[0].rel(semV, cV.v)
                        bks[1].rel(semV, cV.v)
                        hm_last = cV.v
                        gslot_free = (semV, cV.v)
                hmid_guard = 0
                hm_all = hm_last
                dn_s = max(v[0] for v in dn_loaded.values())
                dn_p = max(v[1] for v in dn_loaded.values())
                for tc in range(8):
                    bks = dn_pairs[tc % 2]
                    def ft(t, _tc=tc, _b=bks, _hm=max(hm_all, dn_p), _ds=dn_s):
                        t.wait_ge(semV, _hm)
                        t.wait_ge(semS, _ds)
                        _b[0].acq(t)
                        _b[1].acq(t)
                        n = [0, 0]
                        for mf in range(8):
                            hh = AB(HMH, mf * 1024 + _tc * 128, 128)
                            hl = AB(HML, mf * 1024 + _tc * 128, 128)
                            dh = [AB(DWH, mf * 1024 + nn * 512, 512) for nn in range(2)]
                            dl = [AB(DWL, mf * 1024 + nn * 512, 512) for nn in range(2)]
                            for lhs, which in ((hh, "h"), (hl, "l")):
                                for nn in range(2):
                                    movs = (dh[nn], dl[nn]) if which == "h" else (dh[nn],)
                                    for rhs in movs:
                                        ins = t.matmul(_b[nn].ap, lhs, rhs,
                                                       start=(n[nn] == 0),
                                                       stop=(n[nn] == 3 * KCH - 1))
                                        n[nn] += 1
                        ins.then_inc(semT, 1)
                    sT(ft); cT.add(1)
                    if tc == 7:
                        down_free = cT.v
                    mmtgt = cT.v
                    gtile = half * 8 + tc
                    ccol = comb[:, gtile * 8 + e:gtile * 8 + e + 1]
                    def fv(v, _t=mmtgt, _b=bks, _c=ccol, _tc=tc, _e=e, _cr=comb_ready):
                        v.wait_ge(semT, _t)
                        if _e == 0:
                            v.wait_ge(semV, _cr)
                        for nn in range(2):
                            v.drain()
                            d = OUTH + _tc * 1024 + nn * 512
                            if _e == 0:
                                ins = v.tensor_scalar(A(d, 512), _b[nn].ap, _c, None, ALU.mult)
                            else:
                                ins = v.scalar_tensor_tensor(A(d, 512), _b[nn].ap, _c,
                                                             A(d, 512), ALU.mult, ALU.add)
                        ins.then_inc(semV, 1)
                    sV(fv); cV.add(1)
                    bks[0].rel(semV, cV.v)
                    bks[1].rel(semV, cV.v)
                dw_free = down_free
                def fv(v, _h=down_free):
                    v.wait_ge(semT, _h)
                sV(fv)
            out_done = cV.v
            for tl in range(8):
                gtile = half * 8 + tl
                def fg(g, _gt=gtile, _t=cT.v, _v=cV.v):
                    g.wait_ge(semT, _t)
                    g.wait_ge(semV, _v)
                    gD(g, A(XTS, 1024), src_ap[_gt * 128:(_gt + 1) * 128, :])
                sG(fg)
                gt_ = cG.v
                def fv(v, _g=gt_, _tl=tl, _od=out_done):
                    v.wait_ge(semG, _g)
                    v.wait_ge(semV, _od)
                    v.drain()
                    ins = v.scalar_tensor_tensor(A(XTS, 1024), A(OUTH + _tl * 1024, 1024),
                                                 0.5, A(XTS, 1024), ALU.mult, ALU.add)
                    ins.then_inc(semV, 1)
                sV(fv); cV.add(1)
                vt_ = cV.v
                def fg(g, _v=vt_, _gt=gtile):
                    g.wait_ge(semV, _v)
                    gD(g, dst_ap[_gt * 128:(_gt + 1) * 128, :], A(XTS, 1024))
                sG(fg)
            barrier()

    ffn_phase(xin, xB, arin1, arout1)
    if upto == "ffn1":
        def fg(g):
            for tt in range(TILES):
                gD(g, yout[tt * 128:(tt + 1) * 128, :], xB[tt * 128:(tt + 1) * 128, :])
            g.wait_ge(semG, cG.v)
        sG(fg)
        stack.close()
        return nc

    GW = 0
    ZND = 24576
    AXT = 40960
    AZN = 43008
    ASQ = 45056
    def fy(yq):
        yD(yq, A(GW, 24576), gwp_in)
    sY(fy)
    gw_loaded = cW.v
    xfree = {}
    for tt in range(TILES):
        slot = AXT + (tt % 2) * 1024
        wtgt = xfree.get(tt - 2, 0)
        def fg(g, _tt=tt, _s=slot, _w=wtgt):
            if _w:
                g.wait_ge(semT, _w)
            gD(g, A(_s, 1024), xB[_tt * 128:(_tt + 1) * 128, :])
        sG(fg)
        zdone = ln_tile(A(slot, 1024), A(AZN, 1024), ASQ, wait_g=cG.v)
        def fg(g, _v=zdone, _tt=tt):
            g.wait_ge(semV, _v)
            gD(g, zD[_tt * 128:(_tt + 1) * 128, :], A(AZN, 1024))
        sG(fg)
        zst = cG.v
        nd = ln_tile(A(AZN, 1024), A(slot, 1024), ASQ, wait_g=zst)
        transpose_tile(A(slot, 1024), ZND, 2048, tt, affine=(dng, dnb), wait_v=nd)
        xfree[tt] = cT.v
    barrier()

    if upto == "attnpre":
        def fg(g):
            for tt in range(TILES):
                gD(g, yout[tt * 128:(tt + 1) * 128, :], zD[tt * 128:(tt + 1) * 128, :])
            g.wait_ge(semG, cG.v)
        sG(fg)
        stack.close()
        return nc

    AQV = 40960
    AKV = 43008
    AK2 = 45056
    def ft(t, _w=gw_loaded):
        t.wait_ge(semW, _w)
    sT(ft)
    qspill = {}
    for co in range(KCH):
        for nseg in range(4):
            bank = PBK[co % 2]
            def ft(t, _co=co, _n=nseg, _b=bank):
                _b.acq(t)
                for kc in range(KCH):
                    ins = t.matmul(_b.ap,
                                   arena[:, GW + kc * 3072 + _co * 128:GW + kc * 3072 + (_co + 1) * 128],
                                   arena[:, ZND + kc * 2048 + _n * 512:ZND + kc * 2048 + (_n + 1) * 512],
                                   start=(kc == 0), stop=(kc == KCH - 1))
                ins.then_inc(semT, 1)
            sT(ft); cT.add(1)
            mmt = cT.v
            qsl = AQV + (nseg % 2) * 512
            wq = qspill.get((co, nseg - 2) if nseg >= 2 else (co - 1, nseg + 2), 0)
            def fv(v, _t=mmt, _co=co, _b=bank, _q=qsl, _w=wq):
                v.wait_ge(semT, _t)
                if _w:
                    v.wait_ge(semW, _w)
                ins = v.tensor_scalar(A(_q, 512), _b.ap, qbc[:, _co:_co + 1], None, ALU.add)
                ins.then_inc(semV, 1)
            sV(fv); cV.add(1)
            bank.rel(semV, cV.v)
            ev = cV.v
            def fy(yq, _v=ev, _co=co, _n=nseg, _q=qsl):
                yq.wait_ge(semV, _v)
                yq.wait_ge(semW, cW.v) if False else None
                yD(yq, qD[_n * 4:(_n + 1) * 4, :, _co, :].rearrange("s p f -> p s f"),
                   arena[:, _q:_q + 512].rearrange("p (s f) -> p s f", s=4))
            sY(fy)
            qspill[(co, nseg)] = cW.v
    kvbuf_free = [0, 0]
    lr_banks = [PBK[5], PBK[6]]
    for tt in range(TILES):
        buf = AKV if tt % 2 == 0 else AK2
        bw = kvbuf_free[tt % 2]
        def ft(t, _tt=tt, _w=bw):
            if _w:
                t.wait_ge(semG, _w)
            PBK[2].acq(t)
            PBK[3].acq(t)
            for nh in range(2):
                for kc in range(KCH):
                    ins = t.matmul(PBK[2 + nh].ap,
                                   arena[:, ZND + kc * 2048 + _tt * 128:ZND + kc * 2048 + (_tt + 1) * 128],
                                   arena[:, GW + kc * 3072 + 1024 + nh * 512:GW + kc * 3072 + 1024 + (nh + 1) * 512],
                                   start=(kc == 0), stop=(kc == KCH - 1))
            ins.then_inc(semT, 1)
        sT(ft); cT.add(1)
        kmm = cT.v
        def fv(v, _t=kmm, _buf=buf):
            v.wait_ge(semT, _t)
            v.tensor_add(A(_buf, 512), PBK[2].ap, gbb[:, 0:512])
            v.tensor_add(A(_buf + 512, 512), PBK[3].ap, gbb[:, 512:1024])
            v.drain()
            v.tensor_mul(A(44032, 1024), A(_buf, 1024), A(_buf, 1024))
            v.drain()
            v.tensor_reduce(st1, A(44032, 1024), AX.X, ALU.add).then_inc(semV, 1)
        sV(fv); cV.add(1)
        PBK[2].rel(semV, cV.v)
        PBK[3].rel(semV, cV.v)
        def fs(s, _v=cV.v):
            s.wait_ge(semV, _v)
            s.activation(st2, st1, AF.Sqrt, bias=0.0, scale=1.0).then_inc(semS, 1)
        sS(fs); cS.add(1)
        def fv(v, _s=cS.v, _buf=buf):
            v.wait_ge(semS, _s)
            v.reciprocal(st2, st2)
            v.drain()
            ins = v.tensor_scalar(A(_buf, 1024), A(_buf, 1024), st2, None, ALU.mult)
            ins.then_inc(semV, 1)
        sV(fv); cV.add(1)
        knorm = cV.v
        def fg(g, _v=knorm, _tt=tt, _buf=buf):
            g.wait_ge(semV, _v)
            gD(g, kvin[_tt * 128:(_tt + 1) * 128, 0:1024], A(_buf, 1024))
        sG(fg)
        kst = cG.v
        def ft(t, _tt=tt, _g=kst):
            t.wait_ge(semG, _g)
            PBK[2].acq(t)
            PBK[3].acq(t)
            for nh in range(2):
                for kc in range(KCH):
                    ins = t.matmul(PBK[2 + nh].ap,
                                   arena[:, ZND + kc * 2048 + _tt * 128:ZND + kc * 2048 + (_tt + 1) * 128],
                                   arena[:, GW + kc * 3072 + 2048 + nh * 512:GW + kc * 3072 + 2048 + (nh + 1) * 512],
                                   start=(kc == 0), stop=(kc == KCH - 1))
            ins.then_inc(semT, 1)
        sT(ft); cT.add(1)
        vmm = cT.v
        def fv(v, _t=vmm, _buf=buf):
            v.wait_ge(semT, _t)
            v.tensor_add(A(_buf, 512), PBK[2].ap, gbb[:, 1024:1536])
            ins = v.tensor_add(A(_buf + 512, 512), PBK[3].ap, gbb[:, 1536:2048])
            ins.then_inc(semV, 1)
        sV(fv); cV.add(1)
        PBK[2].rel(semV, cV.v)
        PBK[3].rel(semV, cV.v)
        vev = cV.v
        def fg(g, _v=vev, _tt=tt, _buf=buf):
            g.wait_ge(semV, _v)
            gD(g, kvin[_tt * 128:(_tt + 1) * 128, 1024:2048], A(_buf, 1024))
        sG(fg)
        kvbuf_free[tt % 2] = cG.v
        lb = lr_banks[tt % 2]
        def ft(t, _tt=tt, _b=lb):
            _b.acq(t)
            for kc in range(KCH):
                ins = t.matmul(_b.ap[:, 0:1],
                               arena[:, ZND + kc * 2048 + _tt * 128:ZND + kc * 2048 + (_tt + 1) * 128],
                               lrw[:, kc:kc + 1],
                               start=(kc == 0), stop=(kc == KCH - 1))
            ins.then_inc(semT, 1)
        sT(ft); cT.add(1)
        lmm = cT.v
        def fv(v, _t=lmm, _b=lb):
            v.wait_ge(semT, _t)
            ins = v.tensor_add(A(AQV + 1024, 1), _b.ap[:, 0:1], lrbb)
            ins.then_inc(semV, 1)
        sV(fv); cV.add(1)
        lb.rel(semV, cV.v)
        lev = cV.v
        def fg(g, _v=lev, _tt=tt):
            g.wait_ge(semV, _v)
            gD(g, kvin[_tt * 128:(_tt + 1) * 128, 2048:2049], A(AQV + 1024, 1), slow=True)
            if _tt == 7:
                g.wait_ge(semG, cG.v)
                g.collective_compute("AllGather", ALU.bypass,
                                     replica_groups=[[0, 1], [2, 3], [4, 5], [6, 7]],
                                     ins=[kvin[0:1024, :]], outs=[kvoutA]).then_inc(semC, 1)
            if _tt == 15:
                g.wait_ge(semG, cG.v)
                g.collective_compute("AllGather", ALU.bypass,
                                     replica_groups=[[0, 1], [2, 3], [4, 5], [6, 7]],
                                     ins=[kvin[1024:2048, :]], outs=[kvoutB]).then_inc(semC, 1)
        sG(fg)
        if tt == 7:
            cC.add(1)
            kvA_ready = cC.v
        if tt == 15:
            cC.add(1)
            kvB_ready = cC.v
    barrier()

    if upto == "qkv":
        def fg(g):
            g.wait_ge(semC, kvB_ready)
            for tt in range(8):
                gD(g, yout[tt * 128:(tt + 1) * 128, 0:1024],
                   kvoutA[tt * 128:(tt + 1) * 128, 0:1024])
            for tt in range(8):
                gD(g, yout[(8 + tt) * 128:(9 + tt) * 128, 0:1024],
                   kvoutB[tt * 128:(tt + 1) * 128, 0:1024])
            g.wait_ge(semG, cG.v)
        sG(fg)
        stack.close()
        return nc

    S_ = 0
    GT_ = 4096
    KVT = 8192
    FMK2 = [14352, 15376, 16400]
    LPW2 = [17424, 19216]
    PCH2 = [21008, 21136, 21264]
    RB = 21392
    UB = 21904
    U2B = 22416
    VMB2 = [22928, 23440, 23952]
    OWPF = 24576
    def fy(yq):
        yD(yq, A(OWPF, 8192), owp_in)
    sY(fy)
    ow_loaded = cW.v

    kv_rel = {}
    pc_state = {}
    r_t = {}
    ld_t = {}
    vm_t = {}
    fmk_t = {}
    iat_t = {}
    u_t = {}
    s_ready = 0
    gt_last = 0

    def prep_load(j):
        sl = j % 3
        rel = kv_rel.get(j - 3, 0)
        rank, loc = j // 16, j % 16
        kv_src = (kvoutA if loc < 8 else kvoutB)
        kv_row = rank * 1024 + (loc % 8) * 128
        kv_sem = kvA_ready if loc < 8 else kvB_ready
        def fy(yq, _s=sl, _r=rel, _src=kv_src, _row=kv_row, _c=kv_sem):
            if _r:
                yq.wait_ge(semT, _r)
            yq.wait_ge(semC, _c)
            yD(yq, A(KVT + _s * 2052, 2049), _src[_row:_row + 128, 0:2049])
        sY(fy)
        ld_t[j] = cW.v
        vt0 = arena[:, KVT + sl * 2052 + 1024:KVT + sl * 2052 + 1536]
        vt1 = arena[:, KVT + sl * 2052 + 1536:KVT + sl * 2052 + 2048]
        def fv(v, _ld=ld_t[j], _v0=vt0, _v1=vt1, _m=VMB2[sl]):
            v.wait_ge(semW, _ld)
            v.drain()
            v.tensor_sub(A(_m, 512), _v1, _v0)
            v.drain()
            ins = v.scalar_tensor_tensor(A(_m, 512), A(_m, 512), par, _v0,
                                         ALU.mult, ALU.add)
            ins.then_inc(semV, 1)
        sV(fv); cV.add(1)
        vm_t[j] = cV.v

    def prep_compute1(j):
        sl = j % 3
        kt = arena[:, KVT + sl * 2052:KVT + sl * 2052 + 1024]
        FMK = FMK2[sl]
        LPW = LPW2[j % 2]
        fmk_v, fmk_s = 0, 0
        for gi, bi in enumerate((5, 6)):
            bank = PBK[bi]
            def ft(t, _g=gi, _b=bank, _kt=kt, _ld=ld_t[j]):
                t.wait_ge(semW, _ld)
                _b.acq(t)
                for jj in range(4):
                    kc = _g * 4 + jj
                    ins = t.transpose(_b.ap[:, jj * 128:(jj + 1) * 128],
                                      _kt[:, kc * 128:(kc + 1) * 128], ident)
                ins.then_inc(semT, 1)
            sT(ft); cT.add(1)
            ttgt = cT.v
            def fe(e, _g=gi, _b=bank, _t=ttgt, _F=FMK):
                e.wait_ge(semT, _t)
                for jj in range(4):
                    kc = _g * 4 + jj
                    if _g == 0:
                        ins = e.tensor_copy(A(_F + kc * 128, 128),
                                            _b.ap[:, jj * 128:(jj + 1) * 128])
                    else:
                        ins = e.copy(A(_F + kc * 128, 128),
                                     _b.ap[:, jj * 128:(jj + 1) * 128])
                ins.then_inc(semV if _g == 0 else semS, 1)
            if gi == 0:
                sV(fe); cV.add(1)
                bank.rel(semV, cV.v)
                fmk_v = cV.v
            else:
                sS(fe); cS.add(1)
                bank.rel(semS, cS.v)
                fmk_s = cS.v
        fmk_t[j] = (fmk_v, fmk_s)
        def ft(t, _fv=fmk_v, _fs=fmk_s, _F=FMK):
            t.wait_ge(semV, _fv)
            t.wait_ge(semS, _fs)
            PBK[4].acq(t)
            for kc in range(KCH):
                ins = t.matmul(PBK[4].ap[:, 0:128], A(_F + kc * 128, 128),
                               A(_F + kc * 128, 128),
                               start=(kc == 0), stop=(kc == KCH - 1))
            ins.then_inc(semT, 1)
        sT(ft); cT.add(1)
        def fv(v, _t=cT.v, _L=LPW):
            v.wait_ge(semT, _t)
            v.drain()
            ins = v.tensor_mul(A(_L, 128), PBK[4].ap[:, 0:128], tril)
            ins.then_inc(semV, 1)
        sV(fv); cV.add(1)
        PBK[4].rel(semV, cV.v)
        l_ready = cV.v
        def ft(t, _v=l_ready, _L=LPW):
            t.wait_ge(semV, _v)
            PBK[7].acq(t)
            t.transpose(PBK[7].ap[:, 0:128], A(_L, 128), ident).then_inc(semT, 1)
        sT(ft); cT.add(1)
        def fs(sx, _t=cT.v, _L=LPW):
            sx.wait_ge(semT, _t)
            sx.copy(A(_L + 128, 128), PBK[7].ap[:, 0:128]).then_inc(semS, 1)
        sS(fs); cS.add(1)
        PBK[7].rel(semS, cS.v)
        pc_state[j] = (l_ready, cS.v)

    pw_state = {}

    def pc2a_level(j, lev):
        LPW = LPW2[j % 2]
        if lev == 0:
            l_ready, u_ready = pc_state[j]
            pw_state[j] = (LPW, LPW + 128, l_ready, u_ready)
        prevL, prevU, pw_v, pw_s = pw_state[j]
        newL = LPW + (2 + 2 * lev) * 128
        newU = LPW + (3 + 2 * lev) * 128
        def ft(t, _pL=prevL, _pU=prevU, _v=pw_v, _s=pw_s):
            t.wait_ge(semV, _v)
            t.wait_ge(semS, _s)
            PBK[7].acq(t)
            PBK[4].acq(t)
            t.matmul(PBK[7].ap[:, 128:256], A(_pU, 128), A(_pL, 128),
                     start=True, stop=True)
            ins = t.matmul(PBK[4].ap[:, 128:256], A(_pL, 128), A(_pU, 128),
                           start=True, stop=True)
            ins.then_inc(semT, 1)
        sT(ft); cT.add(1)
        mmt = cT.v
        def fv(v, _t=mmt, _nL=newL):
            v.wait_ge(semT, _t)
            v.drain()
            ins = v.tensor_copy(A(_nL, 128), PBK[7].ap[:, 128:256])
            ins.then_inc(semV, 1)
        sV(fv); cV.add(1)
        PBK[7].rel(semV, cV.v)
        def fs(sx, _t=mmt, _nU=newU):
            sx.wait_ge(semT, _t)
            ins = sx.copy(A(_nU, 128), PBK[4].ap[:, 128:256])
            ins.then_inc(semS, 1)
        sS(fs); cS.add(1)
        PBK[4].rel(semS, cS.v)
        pw_state[j] = (newL, newU, cV.v, cS.v)

    def pc2b_level(j, lev):
        LPW = LPW2[j % 2]
        PCH = PCH2[j % 3]
        if lev == 0:
            _, u_ready = pc_state[j]
            def fv(v, _s=u_ready, _L=LPW, _P=PCH):
                v.wait_ge(semS, _s)
                v.drain()
                ins = v.tensor_sub(A(_P, 128), ident, A(_L + 128, 128))
                ins.then_inc(semV, 1)
            sV(fv); cV.add(1)
            iat_t[j] = cV.v
            return
        lv = lev - 1
        Lp = LPW + (2 + 2 * lv) * 128
        p_ready = iat_t[j]
        pw_v = pw_state[j][2]
        def ft(t, _Lp=Lp, _p=p_ready, _pv=pw_v, _P=PCH):
            t.wait_ge(semV, max(_p, _pv))
            PBK[7].acq(t)
            ins = t.matmul(PBK[7].ap[:, 256:384], A(_Lp, 128), A(_P, 128),
                           start=True, stop=True)
            ins.then_inc(semT, 1)
        sT(ft); cT.add(1)
        mmt = cT.v
        def fv(v, _t=mmt, _P=PCH):
            v.wait_ge(semT, _t)
            v.drain()
            ins = v.tensor_add(A(_P, 128), A(_P, 128), PBK[7].ap[:, 256:384])
            ins.then_inc(semV, 1)
        sV(fv); cV.add(1)
        PBK[7].rel(semV, cV.v)
        iat_t[j] = cV.v

    def main_kS(i):
        nonlocal s_ready
        sl = i % 3
        FMK = FMK2[sl]
        if i > 0:
            def ft(t, _sv=s_ready, _f=fmk_t[i], _F=FMK):
                t.wait_ge(semV, max(_sv, _f[0]))
                t.wait_ge(semS, _f[1])
                PBK[2].acq(t)
                for kc in range(KCH):
                    ins = t.matmul(PBK[2].ap, A(_F + kc * 128, 128), A(S_ + kc * 512, 512),
                                   start=(kc == 0), stop=(kc == KCH - 1))
                ins.then_inc(semT, 1)
            sT(ft); cT.add(1)
            ks = cT.v
            def fv(v, _t=ks, _vm=vm_t[i], _m=VMB2[sl]):
                v.wait_ge(semT, _t)
                v.wait_ge(semV, _vm)
                v.drain()
                ins = v.tensor_sub(A(RB, 512), A(_m, 512), PBK[2].ap)
                ins.then_inc(semV, 1)
            sV(fv); cV.add(1)
            PBK[2].rel(semV, cV.v)
        else:
            def fv(v, _vm=vm_t[i], _m=VMB2[sl]):
                v.wait_ge(semV, _vm)
                v.drain()
                ins = v.tensor_copy(A(RB, 512), A(_m, 512))
                ins.then_inc(semV, 1)
            sV(fv); cV.add(1)
        r_t[i] = cV.v

    def main_u(i):
        sl = i % 3
        def ft(t, _p=iat_t[i], _r=r_t[i], _P=PCH2[i % 3]):
            t.wait_ge(semV, max(_p, _r))
            PBK[1].acq(t)
            t.matmul(PBK[1].ap, A(_P, 128), A(RB, 512), start=True, stop=True).then_inc(semT, 1)
        sT(ft); cT.add(1)
        lrc = arena[:, KVT + sl * 2052 + 2048:KVT + sl * 2052 + 2049]
        def fv(v, _t=cT.v, _lr=lrc):
            v.wait_ge(semT, _t)
            v.drain()
            v.tensor_copy(A(UB, 512), PBK[1].ap)
            v.drain()
            ins = v.tensor_scalar(A(U2B, 512), A(UB, 512), _lr, None, ALU.mult)
            ins.then_inc(semV, 1)
        sV(fv); cV.add(1)
        PBK[1].rel(semV, cV.v)
        u_t[i] = cV.v

    def main_S(i, hb):
        nonlocal s_ready
        sl = i % 3
        kt = arena[:, KVT + sl * 2052:KVT + sl * 2052 + 1024]
        def ft(t, _hb=hb, _u=u_t[i], _kt=kt):
            t.wait_ge(semV, _u)
            for j in range(4):
                kc = _hb * 4 + j
                PBK[j].acq(t)
                ins = t.matmul(PBK[j].ap, _kt[:, kc * 128:(kc + 1) * 128], A(UB, 512),
                               start=True, stop=True)
            ins.then_inc(semT, 1)
        sT(ft); cT.add(1)
        mmt = cT.v
        def fv(v, _t=mmt, _hb=hb, _first=(i == 0)):
            v.wait_ge(semT, _t)
            v.drain()
            for j in range(4):
                kc = _hb * 4 + j
                if _first:
                    ins = v.tensor_copy(A(S_ + kc * 512, 512), PBK[j].ap)
                else:
                    ins = v.tensor_add(A(S_ + kc * 512, 512), A(S_ + kc * 512, 512),
                                       PBK[j].ap)
            ins.then_inc(semV, 1)
        sV(fv); cV.add(1)
        for j in range(4):
            PBK[j].rel(semV, cV.v)
        if hb == 1:
            s_ready = cV.v

    def main_G(i, hb):
        nonlocal gt_last
        sl = i % 3
        kt = arena[:, KVT + sl * 2052:KVT + sl * 2052 + 1024]
        def ft(t, _hb=hb, _u=u_t[i], _kt=kt):
            t.wait_ge(semV, _u)
            for j in range(4):
                idx = _hb * 4 + j
                cv, nh = idx // 2, idx % 2
                PBK[j].acq(t)
                ins = t.matmul(PBK[j].ap, A(U2B + cv * 128, 128),
                               _kt[:, nh * 512:(nh + 1) * 512], start=True, stop=True)
            ins.then_inc(semT, 1)
        sT(ft); cT.add(1)
        mmt = cT.v
        def fv(v, _t=mmt, _hb=hb, _first=(i == 0)):
            v.wait_ge(semT, _t)
            v.drain()
            for j in range(4):
                idx = _hb * 4 + j
                cv, nh = idx // 2, idx % 2
                dstp = A(GT_ + cv * 1024 + nh * 512, 512)
                if _first:
                    ins = v.tensor_copy(dstp, PBK[j].ap)
                else:
                    ins = v.tensor_add(dstp, dstp, PBK[j].ap)
            ins.then_inc(semV, 1)
        sV(fv); cV.add(1)
        for j in range(4):
            PBK[j].rel(semV, cV.v)
        if hb == 1:
            gt_last = cV.v
        kv_rel[i] = cT.v

    for j in range(2):
        prep_load(j)
        prep_compute1(j)
        for lev in range(6):
            pc2a_level(j, lev)
        for lev in range(7):
            pc2b_level(j, lev)
    for i in range(NCHUNK):
        if i + 2 < NCHUNK:
            prep_load(i + 2)
        main_kS(i)
        if i + 2 < NCHUNK:
            prep_compute1(i + 2)
        main_u(i)
        if i + 1 >= 2 and i + 1 < NCHUNK:
            for lev in range(4):
                pc2b_level(i + 1, lev)
        main_S(i, 0)
        if i + 1 >= 2 and i + 1 < NCHUNK:
            for lev in range(4, 7):
                pc2b_level(i + 1, lev)
        main_S(i, 1)
        if i + 2 < NCHUNK:
            pc2a_level(i + 2, 0)
            pc2a_level(i + 2, 1)
        main_G(i, 0)
        if i + 2 < NCHUNK:
            pc2a_level(i + 2, 2)
            pc2a_level(i + 2, 3)
        main_G(i, 1)
        if i + 2 < NCHUNK:
            pc2a_level(i + 2, 4)
            pc2a_level(i + 2, 5)
    def fg(g, _v=gt_last):
        g.wait_ge(semV, _v)
        for cv4 in range(4):
            gD(g, gtin[cv4 * 128:(cv4 + 1) * 128, :], A(GT_ + cv4 * 1024, 1024))
        g.wait_ge(semG, cG.v)
        g.collective_compute("AllGather", ALU.bypass,
                             replica_groups=[[0, 1], [2, 3], [4, 5], [6, 7]],
                             ins=[gtin], outs=[gtout]).then_inc(semC, 1)
    sG(fg); cC.add(1)
    gt_done = cC.v
    barrier()

    if upto == "solve":
        def fg(g):
            g.wait_ge(semC, gt_done)
            for tt in range(8):
                gD(g, yout[tt * 128:(tt + 1) * 128, :], gtout[tt * 128:(tt + 1) * 128, :])
            g.wait_ge(semG, cG.v)
        sG(fg)
        stack.close()
        return nc

    GTF = 0
    M2_ = 8192
    QT_ = 16384
    ZT_ = 18432
    AOE = 20480
    def fg(g, _c=gt_done):
        g.wait_ge(semC, _c)
        for cv in range(KCH):
            gD(g, A(GTF + cv * 1024, 1024), gtout[cv * 128:(cv + 1) * 128, :])
    sG(fg)
    gtf_ld = cG.v
    for ck in range(KCH):
        for nh in range(2):
            bank = PBK[ck % 2]
            def ft(t, _ck=ck, _nh=nh, _b=bank, _g=gtf_ld, _w=ow_loaded):
                t.wait_ge(semG, _g)
                t.wait_ge(semW, _w)
                _b.acq(t)
                for cv in range(KCH):
                    ins = t.matmul(_b.ap,
                                   arena[:, GTF + cv * 1024 + _ck * 128:GTF + cv * 1024 + (_ck + 1) * 128],
                                   arena[:, OWPF + cv * 1024 + _nh * 512:OWPF + cv * 1024 + (_nh + 1) * 512],
                                   start=(cv == 0), stop=(cv == KCH - 1))
                ins.then_inc(semT, 1)
            sT(ft); cT.add(1)
            mmt = cT.v
            def fv(v, _t=mmt, _ck=ck, _nh=nh, _b=bank):
                v.wait_ge(semT, _t)
                ins = v.tensor_copy(arena[:, M2_ + _ck * 1024 + _nh * 512:M2_ + _ck * 1024 + (_nh + 1) * 512],
                                    _b.ap)
                ins.then_inc(semV, 1)
            sV(fv); cV.add(1)
            bank.rel(semV, cV.v)
    m2_ready = cV.v
    qrel = {}
    for tt in range(TILES):
        qsl = QT_ + (tt % 2) * 1024
        zsl = ZT_ + (tt % 2) * 1024
        rel = qrel.get(tt - 2, 0)
        def fg(g, _tt=tt, _q=qsl, _z=zsl, _r=rel):
            if _r:
                g.wait_ge(semT, _r)
            gD(g, arena[:, _q:_q + 1024].rearrange("p (c f) -> p c f", c=8),
               qD[_tt])
            gD(g, A(_z, 1024), zD[_tt * 128:(_tt + 1) * 128, :])
        sG(fg)
        ld = cG.v
        aoe_done = 0
        for nh in range(2):
            bank = PBK[2 + nh]
            def ft(t, _q=qsl, _nh=nh, _b=bank, _ld=ld, _m2=m2_ready):
                t.wait_ge(semG, _ld)
                t.wait_ge(semV, _m2)
                _b.acq(t)
                for ck in range(KCH):
                    ins = t.matmul(_b.ap,
                                   arena[:, _q + ck * 128:_q + (ck + 1) * 128],
                                   arena[:, M2_ + ck * 1024 + _nh * 512:M2_ + ck * 1024 + (_nh + 1) * 512],
                                   start=(ck == 0), stop=(ck == KCH - 1))
                ins.then_inc(semT, 1)
            sT(ft); cT.add(1)
            if nh == 1:
                qrel[tt] = cT.v
            mmt = cT.v
            def fv(v, _t=mmt, _nh=nh, _b=bank, _z=zsl):
                v.wait_ge(semT, _t)
                v.drain()
                v.tensor_add(A(AOE + _nh * 512, 512), _b.ap, obb[:, _nh * 512:(_nh + 1) * 512])
                v.drain()
                ins = v.tensor_add(A(AOE + _nh * 512, 512), A(AOE + _nh * 512, 512),
                                   arena[:, _z + _nh * 512:_z + (_nh + 1) * 512])
                ins.then_inc(semV, 1)
            sV(fv); cV.add(1)
            bank.rel(semV, cV.v)
        z1r = cV.v
        def fg(g, _v=z1r, _tt=tt):
            g.wait_ge(semV, _v)
            gD(g, z1D[_tt * 128:(_tt + 1) * 128, :], A(AOE, 1024))
        sG(fg)
        z1s = cG.v
        def fv(v, _g=z1s):
            v.wait_ge(semG, _g)
        sV(fv)
    barrier()

    if upto == "attnout":
        def fg(g):
            for tt in range(TILES):
                gD(g, yout[tt * 128:(tt + 1) * 128, :], z1D[tt * 128:(tt + 1) * 128, :])
            g.wait_ge(semG, cG.v)
        sG(fg)
        stack.close()
        return nc

    W1H = 0
    W1L = 4096
    W2H = 8192
    W2L = 12288
    TFH = 16384
    TFL = 18432
    H1H = 20480
    H1L = 22528
    MZN2 = [24576, 28672]
    Z1B = [32768, 36864]
    XB_ = 40960
    VSCR = 45056
    wsplit = 0
    for j in range(8):
        for (src, hio, loo) in ((w1p_in, W1H, W1L), (w2p_in, W2H, W2L)):
            def fy(yq, _j=j, _src=src, _v=wsplit):
                if _v:
                    yq.wait_ge(semV, _v)
                yD(yq, A(VSCR, 1024), _src[:, _j * 1024:(_j + 1) * 1024])
            sY(fy)
            wt_ = cW.v
            def fv(v, _w=wt_, _j=j, _h=hio, _l=loo):
                v.wait_ge(semW, _w)
                v.drain()
                v.tensor_copy(AB(_h, _j * 1024, 1024), A(VSCR, 1024))
                v.drain()
                ins = v.tensor_tensor(AB(_l, _j * 1024, 1024), A(VSCR, 1024),
                                      AB(_h, _j * 1024, 1024), ALU.subtract)
                ins.then_inc(semV, 1)
            sV(fv); cV.add(1)
            wsplit = cV.v
    w12 = wsplit
    mlp2_pairs = [(PBK[2], PBK[3]), (PBK[0], PBK[1])]
    h1_free = 0
    xb_free = 0
    z1_free = [0, 0]
    for seg in range(4):
        zb = Z1B[seg % 2]
        mz = MZN2[seg % 2]
        def fg(g, _s=seg, _z=zb, _zf=z1_free[seg % 2]):
            if _zf:
                g.wait_ge(semV, _zf)
            gD(g, arena[:, _z:_z + 4096].rearrange("p (t c) -> p t c", t=4),
               z1D[_s * 512:(_s + 1) * 512, :].rearrange("(t p) c -> p t c", p=128))
        sG(fg)
        z1ld = cG.v
        tfv = []
        for tl in range(4):
            nd = ln_tile(A(zb + tl * 1024, 1024), A(mz + tl * 1024, 1024), VSCR,
                         wait_g=z1ld)
            transpose_tile(A(mz + tl * 1024, 1024), 0, 512, tl, wait_v=nd,
                           pair=(TFH, TFL, 512), affine=(mng, mnb, VSCR))
            tfv.append(cV.v)
        tf_done = cV.v
        for hc in range(KCH):
            bank = PBK[hc % 2]
            def ft(t, _hc=hc, _b=bank, _w=w12, _tf=tf_done, _hf=h1_free):
                t.wait_ge(semV, _tf)
                t.wait_ge(semV, _w)
                if _hf:
                    t.wait_ge(semT, _hf)
                _b.acq(t)
                n = 0
                for kc in range(KCH):
                    w1h = AB(W1H, kc * 1024 + _hc * 128, 128)
                    w1l = AB(W1L, kc * 1024 + _hc * 128, 128)
                    th = AB(TFH, kc * 512, 512)
                    tl_ = AB(TFL, kc * 512, 512)
                    for lhs, rhs in ((w1h, th), (w1h, tl_), (w1l, th)):
                        ins = t.matmul(_b.ap, lhs, rhs,
                                       start=(n == 0), stop=(n == 3 * KCH - 1))
                        n += 1
                ins.then_inc(semT, 1)
            sT(ft); cT.add(1)
            mmt = cT.v
            def fv(v, _t=mmt, _hc=hc, _b=bank):
                v.wait_ge(semT, _t)
                v.drain()
                v.tensor_scalar(A(VSCR, 512), _b.ap, b1c[:, _hc:_hc + 1], 0.0,
                                ALU.add, ALU.max)
                v.drain()
                v.tensor_copy(AB(H1H, _hc * 512, 512), A(VSCR, 512))
                v.drain()
                ins = v.tensor_tensor(AB(H1L, _hc * 512, 512), A(VSCR, 512),
                                      AB(H1H, _hc * 512, 512), ALU.subtract)
                ins.then_inc(semV, 1)
            sV(fv); cV.add(1)
            bank.rel(semV, cV.v)
        h1_ready = cV.v
        def fg(g, _s=seg, _x=xb_free):
            if _x:
                g.wait_ge(semV, _x)
            gD(g, arena[:, XB_:XB_ + 4096].rearrange("p (t c) -> p t c", t=4),
               xB[_s * 512:(_s + 1) * 512, :].rearrange("(t p) c -> p t c", p=128))
        sG(fg)
        xbld = cG.v
        for tl in range(4):
            bks = mlp2_pairs[tl % 2]
            def ft(t, _tl=tl, _b=bks, _h=h1_ready):
                t.wait_ge(semV, _h)
                _b[0].acq(t)
                _b[1].acq(t)
                n = [0, 0]
                for hc in range(KCH):
                    hh = AB(H1H, hc * 512 + _tl * 128, 128)
                    hl = AB(H1L, hc * 512 + _tl * 128, 128)
                    wh = [AB(W2H, hc * 1024 + nn * 512, 512) for nn in range(2)]
                    wl = [AB(W2L, hc * 1024 + nn * 512, 512) for nn in range(2)]
                    for lhs, which in ((hh, "h"), (hl, "l")):
                        for nn in range(2):
                            movs = (wh[nn], wl[nn]) if which == "h" else (wh[nn],)
                            for rhs in movs:
                                ins = t.matmul(_b[nn].ap, lhs, rhs,
                                               start=(n[nn] == 0),
                                               stop=(n[nn] == 3 * KCH - 1))
                                n[nn] += 1
                ins.then_inc(semT, 1)
            sT(ft); cT.add(1)
            if tl == 3:
                h1_free = cT.v
            mmt = cT.v
            def fv(v, _t=mmt, _b=bks, _tl=tl, _z=zb, _mz=mz, _xl=xbld):
                v.wait_ge(semT, _t)
                v.wait_ge(semG, _xl)
                for nn in range(2):
                    v.drain()
                    d = _mz + _tl * 1024 + nn * 512
                    v.tensor_add(A(d, 512), _b[nn].ap, b2b[:, nn * 512:(nn + 1) * 512])
                    v.drain()
                    v.tensor_add(A(d, 512), A(d, 512),
                                 arena[:, _z + _tl * 1024 + nn * 512:_z + _tl * 1024 + (nn + 1) * 512])
                    v.drain()
                    ins = v.tensor_add(A(d, 512), A(d, 512),
                                       arena[:, XB_ + _tl * 1024 + nn * 512:XB_ + _tl * 1024 + (nn + 1) * 512])
                ins.then_inc(semV, 1)
            sV(fv); cV.add(1)
            bks[0].rel(semV, cV.v)
            bks[1].rel(semV, cV.v)
            x2r = cV.v
            def fg(g, _v=x2r, _s=seg, _tl=tl, _mz=mz):
                g.wait_ge(semV, _v)
                gD(g, xA[_s * 512 + _tl * 128:_s * 512 + (_tl + 1) * 128, :],
                   A(_mz + _tl * 1024, 1024))
            sG(fg)
        xb_free = cV.v
        z1_free[seg % 2] = cV.v
    barrier()

    if upto == "mlp":
        def fg(g):
            for tt in range(TILES):
                gD(g, yout[tt * 128:(tt + 1) * 128, :], xA[tt * 128:(tt + 1) * 128, :])
            g.wait_ge(semG, cG.v)
        sG(fg)
        stack.close()
        return nc

    ffn_phase(xA, yout, arin2, arout2)

    def fg(g):
        g.wait_ge(semG, cG.v)
    sG(fg)
    stack.close()
    return nc


def _pack_weights(inp):
    w = {}
    gup = np.asarray(inp["L0_ffn_gate_up_proj"], np.float32)
    g = gup.reshape(E, 16, 128, 8, 128)
    g = np.ascontiguousarray(g.transpose(0, 1, 4, 3, 2))
    w["gup_pk"] = g.reshape(E, 16, 128, 8 * 128)
    dwn = np.asarray(inp["L0_ffn_down_proj"], np.float32)
    d = dwn.transpose(0, 2, 1)
    d = d.reshape(E, 8, 128, 1024)
    w["down_pk"] = np.ascontiguousarray(d.transpose(0, 2, 1, 3)).reshape(E, 128, 8 * 1024)
    gw = np.asarray(inp["L0_ffn_gate_weight"], np.float32)
    w["gw_t128"] = np.ascontiguousarray(
        gw.T.reshape(8, 128, 8).transpose(1, 0, 2)).reshape(128, 64)

    def packw(m):
        m = np.asarray(m, np.float32)
        n = m.shape[1]
        return np.ascontiguousarray(m.reshape(8, 128, n).transpose(1, 0, 2)).reshape(128, 8 * n)
    w["gatesw_p"] = packw(inp["L0_attn_gates_w"])
    w["outw_p"] = packw(inp["L0_attn_out_w"])
    w["w1_p"] = packw(inp["L0_attn_w1"])
    w["w2_p"] = packw(inp["L0_attn_w2"])
    gb = np.asarray(inp["L0_attn_gates_b"], np.float32)
    w["gates_b"] = gb.reshape(1, 3072)
    w["qbc"] = np.ascontiguousarray(gb[0:1024].reshape(8, 128).T)
    w["out_b"] = np.asarray(inp["L0_attn_out_b"], np.float32).reshape(1, 1024)
    w["b2"] = np.asarray(inp["L0_attn_b2"], np.float32).reshape(1, 1024)
    w["lr_b"] = np.asarray(inp["L0_attn_lr_b"], np.float32).reshape(1, 1)
    w["lrw_c"] = np.ascontiguousarray(
        np.asarray(inp["L0_attn_lr_w"], np.float32).reshape(8, 128).T)
    w["b1c"] = np.ascontiguousarray(np.asarray(inp["L0_attn_b1"], np.float32).reshape(8, 128).T)
    for nm, key in [("dn_g_c", "L0_attn_dn_g"), ("dn_b_c", "L0_attn_dn_b"),
                    ("mn_g_c", "L0_attn_mn_g"), ("mn_b_c", "L0_attn_mn_b")]:
        w[nm] = np.ascontiguousarray(np.asarray(inp[key], np.float32).reshape(8, 128).T)
    w["simm"] = np.asarray(inp["L0_ffn_similarity_matrix"], np.float32)
    w["ident128"] = np.eye(128, dtype=np.float32)
    w["tril128"] = np.tril(np.ones((128, 128), np.float32), -1)
    w["iota8b"] = np.tile(np.arange(8, dtype=np.float32), (128, 1))
    w["iotaBb"] = w["iota8b"] + BIG
    w["iotap"] = np.arange(128, dtype=np.float32).reshape(128, 1)
    w["epsc"] = np.full((128, 1), EPS, np.float32)
    w["ones1x128"] = np.ones((1, 128), np.float32)
    w["onec"] = np.ones((128, 1), np.float32)
    w["ident8"] = np.eye(8, dtype=np.float32)
    return w


def kernel(**inputs):
    x = np.asarray(inputs["L0_attn_x"], np.float32)
    w = _pack_weights(inputs)
    nc = build_nc("full")
    in_maps = []
    for c in range(NCORES):
        seq, half = c // 2, c % 2
        im = dict(w)
        im["xshard"] = np.ascontiguousarray(x[seq, half * TCORE:(half + 1) * TCORE, :])
        im["parity"] = np.full((128, 1), float(half), np.float32)
        in_maps.append(im)
    res = run_bass_kernel_spmd(nc, in_maps, core_ids=list(range(NCORES)))
    out = np.empty((4, TSEQ, C), np.float32)
    for c in range(NCORES):
        seq, half = c // 2, c % 2
        out[seq, half * TCORE:(half + 1) * TCORE, :] = res.results[c]["y"]
    return out
